# version 07
# baseline (speedup 1.0000x reference)
import numpy as np
import ml_dtypes
from contextlib import ExitStack

import concourse.bass as bass
import concourse.mybir as mybir
from concourse.bass_utils import run_bass_kernel_spmd

F32 = mybir.dt.float32
BF16 = mybir.dt.bfloat16
I32 = mybir.dt.int32
AF = mybir.ActivationFunctionType
OP = mybir.AluOpType
AX = mybir.AxisListType.X
AXC = mybir.AxisListType.C

D = 2048
T = 4096
EPS = 1e-5
NW_GU = 8192 * 2048
ISQ = 1.0 / 64.0
TWO_M25 = 2.0 ** -25
WBAND = 5e-7
WBANDC = WBAND * (1 << 25)
MAG32 = 12582912.0
MAGK = 1.5 * float((1 << 52) - (1 << 23)) * float(1 << 25)
HEDGE_LO = 0.14
HEDGE_HI = 0.30

ENGS = ("sync", "gpsimd", "tensor", "vector", "scalar")


class Sem:
    __slots__ = ("h", "v", "name")


class KB:

    def __init__(self, nc, es):
        self.nc = nc
        self.es = es
        self.plan = {e: [] for e in ENGS}

    def sem(self, name):
        s = Sem()
        s.h = self.es.enter_context(self.nc.semaphore(name))
        s.v = 0
        s.name = name
        return s

    def wait(self, eng, sem, val=None):
        v = sem.v if val is None else val
        if v <= 0:
            return
        self.plan[eng].append(lambda e, h=sem.h, vv=v: e.wait_ge(h, vv))

    def do(self, eng, fn, inc=None, amt=1):
        dr = eng in ("vector", "scalar", "gpsimd")
        if inc is None:
            if dr:
                self.plan[eng].append(lambda e, f=fn: (f(e), e.drain()))
            else:
                self.plan[eng].append(fn)
        else:
            inc.v += amt
            if dr:
                self.plan[eng].append(
                    lambda e, f=fn, h=inc.h, a=amt:
                    (f(e).then_inc(h, a), e.drain()))
            else:
                self.plan[eng].append(
                    lambda e, f=fn, h=inc.h, a=amt: f(e).then_inc(h, a))

    def dma(self, eng, dst, src, sem):
        self.do(eng, lambda e, d=dst, s=src: e.dma_start(out=d, in_=s),
                inc=sem, amt=16)

    def settle(self, sems):
        for s in sems:
            self.wait("sync", s)

    def fence(self, sem, engs=ENGS):
        for p in engs:
            self.do(p, lambda e: e.drain(), inc=sem)
        v = sem.v
        for c in engs:
            self.wait(c, sem, v)


def build_nc(n_phases=7):
    nc = bass.Bass("TRN2", target_bir_lowering=False)

    x_own_t = nc.dram_tensor("x_own_t", [8, 128, 16, 256], F32,
                             kind="ExternalInput").ap()
    x_full_t = nc.dram_tensor("x_full_t", [16, 128, 16, 256], F32,
                              kind="ExternalInput").ap()
    wq_t = nc.dram_tensor("wq_t", [16, 128, 16, 128], F32,
                          kind="ExternalInput").ap()
    wk_t = nc.dram_tensor("wk_t", [16, 128, 16, 128], F32,
                          kind="ExternalInput").ap()
    wv_t = nc.dram_tensor("wv_t", [16, 128, 16, 128], F32,
                          kind="ExternalInput").ap()
    wo_t = nc.dram_tensor("wo_t", [16, 128, 16, 128], F32,
                          kind="ExternalInput").ap()
    wg_t = nc.dram_tensor("wg_t", [64, 128, 16, 128], F32,
                          kind="ExternalInput").ap()
    wu_t = nc.dram_tensor("wu_t", [64, 128, 16, 128], F32,
                          kind="ExternalInput").ap()
    wd_t = nc.dram_tensor("wd_t", [16, 128, 64, 128], F32,
                          kind="ExternalInput").ap()
    masks = nc.dram_tensor("masks", [128, 2, 512], BF16,
                           kind="ExternalInput").ap()
    identc = nc.dram_tensor("identc", [128, 128], BF16,
                            kind="ExternalInput").ap()
    identfc = nc.dram_tensor("identfc", [128, 128], F32,
                             kind="ExternalInput").ap()
    onesc = nc.dram_tensor("onesc", [128, 1], BF16,
                           kind="ExternalInput").ap()
    onesrc = nc.dram_tensor("onesrc", [1, 128], F32,
                            kind="ExternalInput").ap()
    out_t = nc.dram_tensor("out_t", [16, 128, 2048], F32,
                           kind="ExternalOutput").ap()

    q_sp = nc.dram_tensor("q_sp", [16, 128, 2048], BF16, kind="Internal").ap()
    k_sp = nc.dram_tensor("k_sp", [16, 128, 4096], BF16, kind="Internal").ap()
    v_sp = nc.dram_tensor("v_sp", [16, 128, 4096], BF16, kind="Internal").ap()
    r1_sp = nc.dram_tensor("r1_sp", [16, 128, 2048], F32, kind="Internal").ap()
    gu_sp = nc.dram_tensor("gu_sp", [64, 128, 2048], F32, kind="Internal").ap()
    wdq_sp = nc.dram_tensor("wdq_sp", [16, 128, 8192], BF16,
                            kind="Internal").ap()

    with ExitStack() as es:
        kb = KB(nc, es)
        sb = lambda st, name, shape, dt: st.enter_context(
            nc.sbuf_tensor(name, shape, dt))
        pst = lambda st, name, shape, dt=F32: st.enter_context(
            nc.psum_tensor(name, shape, dt))

        identb = sb(es, "identb", [128, 128], BF16)
        identf = sb(es, "identf", [128, 128], F32)
        onesb = sb(es, "onesb", [128, 1], BF16)
        onesr = sb(es, "onesr", [1, 128], F32)
        cols = sb(es, "cols", [128, 32], F32)
        bcols = sb(es, "bcols", [128, 8], BF16)
        wacc = sb(es, "wacc", [128, 192], F32)
        wscol = sb(es, "wscol", [128, 16], F32)
        gmax = sb(es, "gmax", [128, 64], F32)
        M16 = sb(es, "M16", [128, 64], F32)

        S = {}
        for n in ("const w sq sqc tst vrow u sg h1 mmps ev wbf bfree "
                  "vt vtev sps sevS sevV mx exp vn ct ctev pt ptev ptd pv "
                  "o2 g wabf abs r1 prep h2q q2 q2g par acc "
                  "gps ups gfree ufree sil gu guv wdq wdqd xq2 "
                  "gmx wdb0 wdb1 wqs wqv "
                  "dps dfree oev f "
                  "x0 x1 w0 w1 k0 k1 ql0 ql1 vv dq0 dq1 dkv0 dkv1 "
                  "wab0 wab1 xc0 xc1 r1d0 r1d1 rr0 rr1 wg0 wg1 wu0 wu1 "
                  "gud0 gud1 gurd0 gurd1 wdl0 wdl1 rc0 ow0").split():
            S[n] = kb.sem(n)

        kb.dma("sync", identb[:, :], identc[:, :], S["const"])
        kb.dma("sync", identf[:, :], identfc[:, :], S["const"])
        kb.dma("sync", onesb[:, :], onesc[:, :], S["const"])
        kb.dma("sync", onesr[0:1, :], onesrc[:, :], S["const"])
        for e in ("tensor", "vector"):
            kb.wait(e, S["const"])

        def plan_ln(get_chunk, W, wr_fn, scr, rows, ps_sx, ps_sq,
                    ps_mb, ps_rb, evb, key):
            xb = [scr["xb"][:, 0, :], scr["xb"][:, 1, :]]
            sqb = [scr["sqb"][:, 0, :], scr["sqb"][:, 1, :]]
            t1 = scr["t1"]
            m_row, t_row = rows[0:1, 0, :], rows[0:1, 1, :]
            s_row, r_row = rows[0:1, 2, :], rows[0:1, 3, :]
            cast_at = []
            for i in range(16):
                if i >= 2:
                    kb.wait("scalar", S["tst"], cast_at[i - 2])
                kb.do("scalar", lambda e, i=i, b=xb[i % 2]: e.copy(
                    b, get_chunk(i)))
                kb.do("scalar", lambda e, i=i, b=sqb[i % 2]: e.square(
                    b, get_chunk(i)), inc=S["sq"])
                kb.wait("tensor", S["sq"])
                if i == 0:
                    kb.wait("tensor", S["vrow"], evb.get((key, "rows"), 0))
                kb.do("tensor", lambda e, i=i, b=xb[i % 2]: e.matmul(
                    ps_sx, onesb[:, :], b, start=(i == 0), stop=(i == 15)))
                kb.do("tensor", lambda e, i=i, b=sqb[i % 2]: e.matmul(
                    ps_sq, onesb[:, :], b, start=(i == 0), stop=(i == 15)),
                    inc=S["tst"])
                cast_at.append(S["tst"].v)
            kb.do("tensor", lambda e: e.drain(), inc=S["mmps"])
            kb.wait("vector", S["mmps"])
            kb.do("vector", lambda e: e.tensor_scalar_mul(
                m_row, ps_sx, 1.0 / D))
            kb.do("vector", lambda e: e.tensor_scalar_mul(
                t_row, ps_sq, 1.0 / D))
            kb.do("vector", lambda e: e.tensor_mul(s_row, m_row, m_row))
            kb.do("vector", lambda e: e.tensor_sub(t_row, t_row, s_row))
            kb.do("vector", lambda e: e.tensor_scalar_add(
                t_row, t_row, EPS), inc=S["u"])
            kb.wait("scalar", S["u"])
            kb.do("scalar", lambda e: e.sqrt(s_row, t_row), inc=S["sg"])
            kb.wait("vector", S["sg"])
            kb.do("vector", lambda e: e.reciprocal(r_row, s_row))
            kb.do("vector", lambda e: e.tensor_mul(s_row, r_row, r_row))
            kb.do("vector", lambda e: e.tensor_mul(s_row, s_row, t_row))
            kb.do("vector", lambda e: e.tensor_scalar(
                s_row, s_row, -0.5, 1.5, OP.mult, OP.add))
            kb.do("vector", lambda e: e.tensor_mul(r_row, r_row, s_row),
                  inc=S["vrow"])
            evb[(key, "rows")] = S["vrow"].v
            kb.wait("tensor", S["vrow"])
            kb.wait("tensor", S["h1"], evb.get((key, "bc"), 0))
            kb.do("tensor", lambda e: e.matmul(
                ps_mb, onesr[0:1, :], m_row, start=True, stop=True))
            kb.do("tensor", lambda e: e.matmul(
                ps_rb, onesr[0:1, :], r_row, start=True, stop=True),
                inc=S["mmps"])
            kb.wait("vector", S["mmps"])
            for i in range(16):
                kb.do("vector", lambda e, i=i: e.tensor_sub(
                    t1, get_chunk(i), ps_mb))
                kb.do("vector", lambda e, i=i: wr_fn(e, i, t1, ps_rb),
                      inc=(S["h1"] if i == 15 else None))
            evb[(key, "bc")] = S["h1"].v

        with ExitStack() as aes:
            xbuf = sb(aes, "xbuf", [128, 2, 16, 256], F32)
            wbuf = sb(aes, "wbuf", [128, 2, 2048], BF16)
            ascr = {
                "xb": sb(aes, "a_xb", [128, 2, 256], BF16),
                "sqb": sb(aes, "a_sqb", [128, 2, 256], BF16),
                "t1": sb(aes, "a_t1", [128, 256], F32)[:, :],
            }
            arows = sb(aes, "a_rows", [1, 4, 256], F32)
            ps_a = pst(aes, "ps_a", [128, 8, 512])
            evb = {}

            def ln_banks():
                return (ps_a[0:1, 6, 0:256], ps_a[0:1, 7, 0:256],
                        ps_a[:, 4, 0:256], ps_a[:, 5, 0:256])

            with ExitStack() as a1es:
                h1o = sb(a1es, "h1o", [128, 8, 16, 256], BF16)
                qstage = sb(a1es, "qstage", [128, 2, 2048], BF16)

                for st in range(8):
                    slot = st % 2
                    sx = S["x%d" % slot]
                    kb.wait("sync", S["h1"], max(0, S["h1"].v - 1))
                    kb.dma("sync", xbuf[:, slot, :, :], x_own_t[st], sx)
                    kb.wait("vector", sx)
                    kb.wait("scalar", sx)
                    gc = lambda i, slot=slot: xbuf[:, slot, i, :]

                    def wr_own(e, i, t1, rps, st=st):
                        return e.tensor_mul(h1o[:, st, i, :], t1, rps)
                    p_sx, p_sq, p_mb, p_rb = ln_banks()
                    plan_ln(gc, 256, wr_own, ascr, arows,
                            p_sx, p_sq, p_mb, p_rb, evb, "ln")
                h1_own = S["h1"].v

                for f in range(16):
                    fs = f % 2
                    sw = S["w%d" % fs]
                    kb.wait("gpsimd", S["wbf"], evb.get(("wb", fs), 0))
                    kb.do("gpsimd", lambda e, f=f, fs=fs: e.dma_start(
                        out=wbuf[:, fs, :], in_=wq_t[f]), inc=sw, amt=16)
                    kb.wait("tensor", sw)
                    kb.wait("tensor", S["h1"], h1_own)
                    for st8 in range(8):
                        b = st8 % 4
                        kb.wait("tensor", S["ev"], evb.get(("a", b), 0))
                        for i in range(16):
                            kb.do("tensor",
                                  lambda e, fs=fs, st8=st8, i=i, b=b:
                                  e.matmul(ps_a[:, b, 0:256],
                                           wbuf[:, fs, 128 * i:128 * i + 128],
                                           h1o[:, st8, i, :],
                                           start=(i == 0), stop=(i == 15)),
                                  inc=(S["mmps"] if i == 15 else None))
                        kb.wait("vector", S["mmps"])
                        kb.wait("vector", S["dq%d" % fs],
                                evb.get(("q", fs), 0))
                        kb.do("vector", lambda e, fs=fs, st8=st8, b=b:
                              e.tensor_copy(
                                  qstage[:, fs, 256 * st8:256 * st8 + 256],
                                  ps_a[:, b, 0:256]), inc=S["ev"])
                        evb[("a", b)] = S["ev"].v
                    kb.do("tensor", lambda e: e.drain(), inc=S["wbf"])
                    evb[("wb", fs)] = S["wbf"].v
                    kb.wait("sync", S["ev"])
                    kb.dma("sync", q_sp[f], qstage[:, fs, :], S["dq%d" % fs])
                    evb[("q", fs)] = S["dq%d" % fs].v

            if n_phases >= 2:
                kb.settle([S["dq0"], S["dq1"]])
                kb.fence(S["f"])
                with ExitStack() as a2es:
                    H1 = sb(a2es, "H1", [128, 16, 4096], BF16)
                    kvstage = sb(a2es, "kvstage", [128, 2, 4096], BF16)

                    for st in range(16):
                        slot = st % 2
                        sx = S["x%d" % slot]
                        kb.wait("sync", S["h1"], max(0, S["h1"].v - 1))
                        kb.dma("sync", xbuf[:, slot, :, :], x_full_t[st], sx)
                        kb.wait("vector", sx)
                        kb.wait("scalar", sx)
                        gc = lambda i, slot=slot: xbuf[:, slot, i, :]

                        def wr_full(e, i, t1, rps, st=st):
                            return e.tensor_mul(
                                H1[:, i, 256 * st:256 * st + 256], t1, rps)
                        p_sx, p_sq, p_mb, p_rb = ln_banks()
                        plan_ln(gc, 256, wr_full, ascr, arows,
                                p_sx, p_sq, p_mb, p_rb, evb, "ln")
                    h1_full = S["h1"].v

                    for wt, sp in ((wk_t, k_sp), (wv_t, v_sp)):
                        for f in range(16):
                            fs = f % 2
                            sw = S["w%d" % fs]
                            kb.wait("gpsimd", S["wbf"],
                                    evb.get(("wb", fs), 0))
                            kb.do("gpsimd", lambda e, wt=wt, f=f, fs=fs:
                                  e.dma_start(out=wbuf[:, fs, :], in_=wt[f]),
                                  inc=sw, amt=16)
                            kb.wait("tensor", sw)
                            kb.wait("tensor", S["h1"], h1_full)
                            for g in range(2):
                                for strip in range(4):
                                    kb.wait("tensor", S["ev"],
                                            evb.get(("a", 4 * g + strip), 0))
                                for i in range(16):
                                    for strip in range(4):
                                        col = 2048 * g + 512 * strip
                                        kb.do("tensor",
                                              lambda e, fs=fs, i=i, g=g,
                                              strip=strip, col=col:
                                              e.matmul(
                                                  ps_a[:, 4 * g + strip, :],
                                                  wbuf[:, fs,
                                                       128 * i:128 * i + 128],
                                                  H1[:, i, col:col + 512],
                                                  start=(i == 0),
                                                  stop=(i == 15)),
                                              inc=(S["mmps"]
                                                   if (i == 15 and strip == 3)
                                                   else None))
                                kb.wait("vector", S["mmps"])
                                kb.wait("vector", S["dkv%d" % fs],
                                        evb.get(("kv", fs), 0))
                                for strip in range(4):
                                    col = 2048 * g + 512 * strip
                                    kb.do("vector",
                                          lambda e, fs=fs, col=col, g=g,
                                          strip=strip:
                                          e.tensor_copy(
                                              kvstage[:, fs, col:col + 512],
                                              ps_a[:, 4 * g + strip, :]),
                                          inc=S["ev"])
                                    evb[("a", 4 * g + strip)] = S["ev"].v
                            kb.do("tensor", lambda e: e.drain(),
                                  inc=S["wbf"])
                            evb[("wb", fs)] = S["wbf"].v
                            kb.wait("sync", S["ev"])
                            kb.dma("sync", sp[f], kvstage[:, fs, :],
                                   S["dkv%d" % fs])
                            evb[("kv", fs)] = S["dkv%d" % fs].v

        if n_phases >= 3:
            kb.settle([S["dkv0"], S["dkv1"]])
            kb.fence(S["f"])
            with ExitStack() as o2s:
                out2 = sb(o2s, "out2", [128, 16, 2048], BF16)
                _phase_B(kb, S, sb, pst, o2s, out2, masks, identb, onesb,
                         cols, bcols, wacc, q_sp, k_sp, v_sp, wg_t, wu_t)
                if n_phases >= 4:
                    kb.fence(S["f"])
                    _phase_C1a(kb, S, sb, pst, o2s, out2, wo_t, x_own_t,
                               r1_sp)
            if n_phases >= 5:
                kb.fence(S["f"])
                with ExitStack() as rs:
                    rs2b = sb(rs, "rs2b", [128, 4, 512], F32)
                    s2bd = sb(rs, "s2bd", [128, 4, 512], F32)
                    with ExitStack() as hs:
                        h2q = sb(hs, "h2q", [128, 16, 2048], BF16)
                        sbg = sb(hs, "sbg", [128, 4, 512], F32)
                        sbu = sb(hs, "sbu", [128, 4, 512], F32)
                        sball = sb(hs, "sball", [128, 4, 512], F32)
                        h2q_at = _phase_C1b(
                            kb, S, sb, pst, hs, h2q, sball, wscol,
                            onesb, onesr, r1_sp, plan_ln)
                        if n_phases >= 6:
                            kb.fence(S["f"])
                            s2_at = _phase_C2(
                                kb, S, sb, pst, hs, h2q, sbg, sbu, sball,
                                rs2b, s2bd, wacc, wscol, onesr, identf,
                                gmax, M16, wg_t, wu_t, wd_t, gu_sp,
                                h2q_at)
                    if n_phases >= 7:
                        kb.fence(S["f"])
                        _phase_C3(kb, S, sb, pst, rs, rs2b, s2bd,
                                  gu_sp, wdq_sp, r1_sp, out_t, s2_at,
                                  wd_t, wscol, M16)

        _emit(nc, kb)
    return nc


def _phase_B(kb, S, sb, pst, st_, out2, masks, identb, onesb, cols, bcols,
             wacc, q_sp, k_sp, v_sp, wg_t, wu_t):
    with ExitStack() as bes:
        maskb = sb(bes, "maskb", [128, 2, 512], BF16)
        trow = sb(bes, "trow", [1, 1024], F32)
        g32 = sb(bes, "g32", [1, 1024], F32)
        gbf = sb(bes, "gbf", [1, 1024], BF16)
        crow = sb(bes, "crow", [1, 512], BF16)
        kbuf = sb(bes, "kbuf", [128, 2, 4096], BF16)
        vbuf = sb(bes, "vbuf", [128, 4096], BF16)
        qbuf = sb(bes, "qbuf", [128, 2, 2048], BF16)
        vtm = sb(bes, "vtm", [128, 32, 128], BF16)
        Ssb = sb(bes, "Ssb", [128, 2, 512], F32)
        Psb = sb(bes, "Psb", [128, 2, 4096], BF16)
        PT = sb(bes, "PT", [128, 32, 512], BF16)
        wab = sb(bes, "wab", [128, 2048], F32)
        ps_b = pst(bes, "ps_b", [128, 4, 512])
        ps_bt = pst(bes, "ps_bt", [128, 2, 1024], BF16)
        ps_sm = pst(bes, "ps_sm", [1, 1024])

        BK_S, BK_T, BK_O = (0, 1), (0, 1), (2, 3)

        bfree_at = [0]
        loads_at = {}
        evb = {}

        kb.dma("sync", maskb[:, :, :], masks[:, :, :], S["const"])
        kb.wait("vector", S["const"])

        abs_jobs = [("g", o) for o in range(64)] + \
                   [("u", o) for o in range(64)]
        abs_idx = [0]
        abs_done = []

        def plan_absum_slice():
            for _ in range(8):
                if abs_idx[0] >= len(abs_jobs):
                    return
                mat, o = abs_jobs[abs_idx[0]]
                abs_idx[0] += 1
                if abs_done:
                    kb.wait("sync", S["wabf"], abs_done[-1])
                srcw = wg_t[o] if mat == "g" else wu_t[o]
                kb.dma("sync", wab[:, :], srcw, S["wab0"])
                kb.wait("scalar", S["wab0"])
                col = o if mat == "g" else 64 + o
                kb.do("scalar", lambda e, col=col: e.activation(
                    wab[:, :], wab[:, :], AF.Abs,
                    accum_out=wacc[:, col:col + 1]), inc=S["wabf"])
                abs_done.append(S["wabf"].v)

        def plan_kq_loads(h):
            kb.wait("sync", S["bfree"], bfree_at[0])
            kb.dma("sync", kbuf[:, h % 2, :], k_sp[h], S["k%d" % (h % 2)])
            kb.dma("sync", qbuf[:, h % 2, :], q_sp[h], S["ql%d" % (h % 2)])
            loads_at[("kq", h)] = (S["k%d" % (h % 2)].v,
                                   S["ql%d" % (h % 2)].v)

        def plan_v_load(h):
            kb.wait("sync", S["vtev"], evb.get("vtm_done", 0))
            kb.dma("sync", vbuf[:, :], v_sp[h], S["vv"])
            loads_at[("v", h)] = S["vv"].v

        plan_kq_loads(0)
        plan_v_load(0)
        for h in range(16):
            hs = h % 2
            if h + 1 < 16:
                plan_kq_loads(h + 1)
            plan_absum_slice()

            kv, qv = loads_at[("kq", h)]
            kb.wait("tensor", S["k%d" % hs], kv)
            kb.wait("tensor", S["ql%d" % hs], qv)
            kb.wait("tensor", S["vv"], loads_at[("v", h)])
            for g in range(8):
                bk = BK_T[g % 2]
                kb.wait("tensor", S["vtev"], evb.get(("vt", bk), 0))
                kb.wait("tensor", S["ptev"], evb.get(("pt", bk), 0))
                for kk in range(4):
                    kb.do("tensor", lambda e, g=g, kk=kk, bk=bk: e.transpose(
                        ps_bt[:, bk, 128 * kk:128 * kk + 128],
                        vbuf[:, 512 * g + 128 * kk:512 * g + 128 * kk + 128],
                        identb[:, :]), inc=(S["vt"] if kk == 3 else None))
                kb.wait("vector", S["vt"])
                kb.do("vector", lambda e, g=g, bk=bk: e.tensor_copy(
                    vtm[:, 4 * g:4 * g + 4, :], ps_bt[:, bk, 0:512]),
                    inc=S["vtev"])
                evb[("vt", bk)] = S["vtev"].v
            evb["vtm_done"] = S["vtev"].v
            if h + 1 < 16:
                plan_v_load(h + 1)

            kb.wait("tensor", S["vtev"], evb["vtm_done"])
            kb.wait("tensor", S["vtev"], evb.get("sm_read", 0))
            for c5 in range(8):
                for kk in range(4):
                    kb.do("tensor", lambda e, c5=c5, kk=kk: e.matmul(
                        ps_sm[0:1, 128 * c5:128 * c5 + 128],
                        onesb[:, :], vtm[:, 4 * c5 + kk, :],
                        start=(kk == 0), stop=(kk == 3)),
                        inc=(S["vt"] if (c5 == 7 and kk == 3) else None))
            kb.wait("vector", S["vt"])
            kb.wait("vector", S["pv"])
            kb.do("vector", lambda e: e.tensor_copy(
                trow[0:1, :], ps_sm[0:1, :]), inc=S["vtev"])
            evb["sm_read"] = S["vtev"].v
            kb.do("vector", lambda e: e.memset(g32[0:1, 896:1024], 0.0))
            for gi in range(6, -1, -1):
                kb.do("vector", lambda e, gi=gi: e.tensor_add(
                    g32[0:1, 128 * gi:128 * gi + 128],
                    g32[0:1, 128 * (gi + 1):128 * (gi + 1) + 128],
                    trow[0:1, 128 * (gi + 1):128 * (gi + 1) + 128]))
            kb.do("vector", lambda e: e.tensor_copy(
                gbf[0:1, :], g32[0:1, :]), inc=S["g"])
            g_at = S["g"].v

            def plan_S(s, sub, sl):
                n_k = (2 * s + 1) if sub < 2 else (2 * s + 2)
                qt = 4 * s + sub
                base = 16 * sl
                rmax, mx = cols[:, base:base + 1], cols[:, base + 1:base + 2]
                negb = cols[:, base + 2:base + 3]
                zp_d = cols[:, base + 5:base + 6]
                chunks = [n_k - 1] + list(range(n_k - 1))
                for ci, c in enumerate(chunks):
                    bk = BK_S[ci % 2]
                    kb.wait("tensor", S["sevS"], evb.get(("sS", bk), 0))
                    kb.wait("tensor", S["sevV"], evb.get(("sV", bk), 0))
                    kb.do("tensor", lambda e, hs=hs, qt=qt, c=c, bk=bk:
                          e.matmul(ps_b[:, bk, :],
                                   qbuf[:, hs, 128 * qt:128 * qt + 128],
                                   kbuf[:, hs, 512 * c:512 * c + 512],
                                   start=True, stop=True), inc=S["sps"])
                    if ci == 0:
                        kb.wait("vector", S["sps"])
                        kb.wait("vector", S["exp"],
                                evb.get(("ssb_free", sl), 0))
                        kb.do("vector", lambda e, bk=bk, sub=sub, sl=sl:
                              e.tensor_mul(Ssb[:, sl, :], ps_b[:, bk, :],
                                           maskb[:, sub % 2, :]),
                              inc=S["sevV"])
                        evb[("sV", bk)] = S["sevV"].v
                        kb.do("vector", lambda e, sl=sl, rmax=rmax:
                              e.reduce_max(rmax, Ssb[:, sl, :], AX))
                        kb.do("vector", lambda e, rmax=rmax, mx=mx:
                              e.tensor_scalar_max(mx, rmax, 0.0))
                        kb.do("vector", lambda e, mx=mx, negb=negb:
                              e.tensor_scalar_mul(negb, mx, -ISQ),
                              inc=S["mx"])
                        kb.wait("scalar", S["mx"])
                        kb.wait("scalar", S["ptd"],
                                evb.get(("psb_free", sl), 0))
                        kb.wait("scalar", S["vn"], evb.get(("zp_free", sl),
                                                           0))
                        kb.do("scalar", lambda e, sl=sl, c=c, negb=negb,
                              zp_d=zp_d: e.activation(
                                  Psb[:, sl, 512 * c:512 * c + 512],
                                  Ssb[:, sl, :], AF.Exp,
                                  bias=negb, scale=ISQ, accum_out=zp_d))
                    else:
                        kb.wait("scalar", S["sps"])
                        kb.do("scalar", lambda e, sl=sl, c=c, ci=ci, bk=bk,
                              negb=negb: e.activation(
                                  Psb[:, sl, 512 * c:512 * c + 512],
                                  ps_b[:, bk, :], AF.Exp,
                                  bias=negb, scale=ISQ,
                                  accum_out=cols[:, 16 * sl + 5 + ci:
                                                 16 * sl + 6 + ci]),
                              inc=S["sevS"])
                        evb[("sS", bk)] = S["sevS"].v
                c_col = bcols[:, sl:sl + 1]
                kb.do("scalar", lambda e, mx=mx, c_col=c_col:
                      e.activation(c_col, mx, AF.Exp, bias=0.0, scale=-ISQ),
                      inc=S["exp"])
                evb[("ssb_free", sl)] = S["exp"].v

            def plan_soft(s, sub, sl):
                n_k = (2 * s + 1) if sub < 2 else (2 * s + 2)
                W = 512 * n_k
                nd = float(T - W)
                base = 16 * sl
                Z = cols[:, base + 3:base + 4]
                rz = cols[:, base + 4:base + 5]
                c_col = bcols[:, sl:sl + 1]
                kb.wait("vector", S["exp"])
                for ci in range(1, n_k):
                    dst = Z if ci == n_k - 1 else cols[:, base + 5:base + 6]
                    kb.do("vector", lambda e, base=base, ci=ci, dst=dst:
                          e.tensor_add(dst, cols[:, base + 5:base + 6],
                                       cols[:, base + 5 + ci:
                                            base + 6 + ci]))
                if n_k == 1:
                    kb.do("vector", lambda e, base=base, Z=Z: e.tensor_copy(
                        Z, cols[:, base + 5:base + 6]))
                kb.do("vector", lambda e, nd=nd, c_col=c_col, Z=Z:
                      e.scalar_tensor_tensor(Z, c_col, nd, Z,
                                             OP.mult, OP.add))
                kb.do("vector", lambda e, Z=Z, rz=rz: e.reciprocal(rz, Z))
                kb.do("vector", lambda e, W=W, sl=sl, rz=rz:
                      e.tensor_single_scalar(Psb[:, sl, 0:W],
                                             Psb[:, sl, 0:W], rz, OP.mult))
                kb.wait("vector", S["ct"], evb.get("ct_at", 0))
                kb.do("vector", lambda e, sub=sub, c_col=c_col, rz=rz:
                      e.tensor_mul(bcols[:, 2 + sub:3 + sub], c_col, rz),
                      inc=S["vn"])
                evb[("zp_free", sl)] = S["vn"].v

            def plan_PT(s, sub, sl):
                n_k = (2 * s + 1) if sub < 2 else (2 * s + 2)
                vn_need = S["vn"].v
                for g in range(n_k):
                    bk = BK_T[g % 2]
                    kb.wait("tensor", S["vn"], vn_need)
                    kb.wait("tensor", S["vtev"], evb.get(("vt", bk), 0))
                    kb.wait("tensor", S["ptev"], evb.get(("pt", bk), 0))
                    if g == 0:
                        kb.wait("tensor", S["ctev"])
                    for kk in range(4):
                        kb.do("tensor", lambda e, g=g, kk=kk, bk=bk, sl=sl:
                              e.transpose(
                                  ps_bt[:, bk, 128 * kk:128 * kk + 128],
                                  Psb[:, sl, 512 * g + 128 * kk:
                                      512 * g + 128 * kk + 128],
                                  identb[:, :]),
                              inc=(S["pt"] if kk == 3 else None))
                    kb.wait("vector", S["pt"])
                    if g == 0:
                        kb.wait("vector", S["pv"], evb.get("pv_last", 0))
                    kb.do("vector", lambda e, g=g, sub=sub, bk=bk:
                          e.tensor_copy(
                              PT[:, 4 * g:4 * g + 4,
                                 128 * sub:128 * sub + 128],
                              ps_bt[:, bk, 0:512]), inc=S["ptev"])
                    evb[("pt", bk)] = S["ptev"].v
                kb.do("tensor", lambda e: e.drain(), inc=S["ptd"])
                evb[("psb_free", sl)] = S["ptd"].v

            def plan_stripe_end(s):
                n_lo, n_hi = 2 * s + 1, 2 * s + 2
                kb.do("vector", lambda e, n_lo=n_lo, n_hi=n_hi: e.memset(
                    PT[:, 4 * n_lo:4 * n_hi, 0:256], 0.0), inc=S["ptev"])
                pt_ready = S["ptev"].v
                kb.wait("tensor", S["vn"], S["vn"].v)
                kb.wait("tensor", S["vtev"], evb.get(("vt", 0), 0))
                kb.wait("tensor", S["ptev"], evb.get(("pt", 0), 0))
                kb.wait("tensor", S["ctev"])
                for sub in range(4):
                    kb.do("tensor", lambda e, sub=sub: e.transpose(
                        ps_bt[0:1, 0, 128 * sub:128 * sub + 128],
                        bcols[:, 2 + sub:3 + sub], identb[:, :]),
                        inc=(S["ct"] if sub == 3 else None))
                evb["ct_at"] = S["ct"].v
                kb.wait("vector", S["ct"])
                kb.do("vector", lambda e: e.tensor_copy(
                    crow[0:1, :], ps_bt[0:1, 0, 0:512]), inc=S["ctev"])
                bo = BK_O[s % 2]
                kb.wait("tensor", S["ptev"], pt_ready)
                kb.wait("tensor", S["o2"], evb.get(("o", bo), 0))
                kb.wait("tensor", S["g"], g_at)
                kb.wait("tensor", S["ctev"])
                for c in range(4 * n_hi):
                    kb.do("tensor", lambda e, c=c, bo=bo: e.matmul(
                        ps_b[:, bo, :], vtm[:, c, :], PT[:, c, :],
                        start=(c == 0), stop=False))
                gi1 = n_lo - 1
                st1 = (n_hi == 8)
                kb.do("tensor", lambda e, gi1=gi1, bo=bo, st1=st1: e.matmul(
                    ps_b[:, bo, 0:256],
                    gbf[0:1, 128 * gi1:128 * gi1 + 128],
                    crow[0:1, 0:256], start=False, stop=st1),
                    inc=(S["pv"] if st1 else None))
                if not st1:
                    gi2 = n_hi - 1
                    kb.do("tensor", lambda e, gi2=gi2, bo=bo: e.matmul(
                        ps_b[:, bo, 256:512],
                        gbf[0:1, 128 * gi2:128 * gi2 + 128],
                        crow[0:1, 256:512], start=False, stop=True),
                        inc=S["pv"])
                evb["pv_last"] = S["pv"].v
                kb.wait("vector", S["pv"])
                kb.do("vector", lambda e, h=h, s=s, bo=bo: e.tensor_copy(
                    out2[:, h, 512 * s:512 * s + 512], ps_b[:, bo, :]),
                    inc=S["o2"])
                evb[("o", bo)] = S["o2"].v

            jg0 = evb.get("jg", 0)
            for j in range(17):
                if j < 16:
                    plan_S(j // 4, j % 4, (jg0 + j) % 2)
                if j >= 1:
                    pj = j - 1
                    plan_PT(pj // 4, pj % 4, (jg0 + pj) % 2)
                    if pj % 4 == 3:
                        plan_stripe_end(pj // 4)
                if j < 16:
                    plan_soft(j // 4, j % 4, (jg0 + j) % 2)
            evb["jg"] = jg0 + 16
            kb.do("tensor", lambda e: e.drain(), inc=S["bfree"])
            bfree_at[0] = S["bfree"].v

        while abs_idx[0] < len(abs_jobs):
            plan_absum_slice()


def _phase_C1a(kb, S, sb, pst, st_, out2, wo_t, x_own_t, r1_sp):
    with ExitStack() as ces:
        wob = sb(ces, "wob", [128, 16, 2048], BF16)
        xc = sb(ces, "xc", [128, 2, 512], F32)
        r1c = sb(ces, "r1c", [128, 2, 512], F32)
        ps_c1 = pst(ces, "ps_c1", [128, 2, 512])
        evb = {}

        for f in range(16):
            kb.do("gpsimd", lambda e, f=f: e.dma_start(
                out=wob[:, f, :], in_=wo_t[f]), inc=S["w"], amt=16)
        kb.wait("tensor", S["w"])

        for st in range(4):
            for f in range(16):
                fs = f % 2
                kb.wait("tensor", S["ev"], evb.get(("wps", fs), 0))
                for d in range(16):
                    kb.do("tensor", lambda e, f=f, d=d, st=st, fs=fs:
                          e.matmul(ps_c1[:, fs, :],
                                   wob[:, f, 128 * d:128 * d + 128],
                                   out2[:, d, 512 * st:512 * st + 512],
                                   start=(d == 0), stop=(d == 15)),
                          inc=(S["mmps"] if d == 15 else None))
                kb.wait("sync", S["ev"], evb.get(("xf", fs), 0))
                kb.dma("sync", xc[:, fs, 0:256],
                       x_own_t[2 * st, :, f, :], S["xc%d" % fs])
                kb.dma("sync", xc[:, fs, 256:512],
                       x_own_t[2 * st + 1, :, f, :], S["xc%d" % fs])
                kb.wait("vector", S["mmps"])
                kb.wait("vector", S["xc%d" % fs])
                kb.wait("vector", S["r1d%d" % fs], evb.get(("rd", fs), 0))
                kb.do("vector", lambda e, fs=fs: e.tensor_add(
                    r1c[:, fs, :], ps_c1[:, fs, :], xc[:, fs, :]),
                    inc=S["ev"])
                evb[("wps", fs)] = S["ev"].v
                evb[("xf", fs)] = S["ev"].v
                kb.do("vector", lambda e: e.drain(), inc=S["r1"])
                kb.wait("sync", S["r1"])
                kb.dma("sync", r1_sp[f][:, 512 * st:512 * st + 512],
                       r1c[:, fs, :], S["r1d%d" % fs])
                evb[("rd", fs)] = S["r1d%d" % fs].v
        kb.settle([S["r1d0"], S["r1d1"]])


def _phase_C1b(kb, S, sb, pst, st_, h2q, sball, wscol, onesb, onesr,
               r1_sp, plan_ln):
    with ExitStack() as ces:
        h2buf = sb(ces, "h2buf", [128, 16, 512], F32)
        cscr = {
            "xb": sb(ces, "c_xb", [128, 2, 512], BF16),
            "sqb": sb(ces, "c_sqb", [128, 2, 512], BF16),
            "t1": sb(ces, "c_t1", [128, 512], F32)[:, :],
        }
        crows = sb(ces, "c_rows", [1, 5, 512], F32)
        habs = sb(ces, "habs", [128, 2, 512], F32)
        M1 = sb(ces, "M1", [128, 512], F32)
        rsb = sb(ces, "rsb", [128, 512], F32)
        qt1 = sb(ces, "qt1", [128, 2, 512], F32)
        ps_c = pst(ces, "ps_c", [128, 4, 512])
        evb = {}

        kb.do("vector", lambda e: e.tensor_scalar(
            crows[0:1, 4, 0:1], onesr[0:1, 0:1], 1.0, TWO_M25,
            OP.mult, OP.add))
        kb.do("vector", lambda e: e.tensor_scalar(
            crows[0:1, 4, 0:1], crows[0:1, 4, 0:1], 1.0, -1.0,
            OP.mult, OP.add))
        kb.do("vector", lambda e: e.tensor_scalar(
            crows[0:1, 4, 0:1], crows[0:1, 4, 0:1], MAGK, MAG32,
            OP.mult, OP.add), inc=S["vrow"])
        kb.wait("tensor", S["vrow"])
        kb.do("tensor", lambda e: e.matmul(
            ps_c[:, 0, 0:1], onesr[0:1, :], crows[0:1, 4, 0:1],
            start=True, stop=True), inc=S["mmps"])
        kb.wait("vector", S["mmps"])
        kb.do("vector", lambda e: e.tensor_copy(
            wscol[:, 14:15], ps_c[:, 0, 0:1]), inc=S["acc"])
        kb.wait("tensor", S["acc"])

        for st in range(4):
            for f in range(16):
                fs = f % 2
                kb.wait("sync", S["h2q"], evb.get(("hb", f), 0))
                kb.wait("sync", S["rr%d" % fs], S["rr%d" % fs].v)
                kb.dma("sync", h2buf[:, f, :],
                       r1_sp[f][:, 512 * st:512 * st + 512], S["rr%d" % fs])
                kb.wait("vector", S["rr%d" % fs])
                kb.wait("scalar", S["rr%d" % fs])
            gc = lambda i: h2buf[:, i, :]

            def wr_h2(e, i, t1, rps):
                return e.tensor_mul(h2buf[:, i, :], t1, rps)
            plan_ln(gc, 512, wr_h2, cscr, crows,
                    ps_c[0:1, 2, 0:512], ps_c[0:1, 3, 0:512],
                    ps_c[:, 0, :], ps_c[:, 1, :], evb, "c1b")
            ab_at = []
            for f in range(16):
                if f >= 2:
                    kb.wait("scalar", S["sqc"], ab_at[f - 2])
                kb.wait("scalar", S["h1"])
                kb.do("scalar", lambda e, f=f, b=habs[:, f % 2, :]:
                      e.activation(b, h2buf[:, f, :], AF.Abs), inc=S["sq"])
                kb.wait("vector", S["sq"])
                if f == 0:
                    kb.do("vector", lambda e: e.tensor_copy(
                        M1[:, :], habs[:, 0, :]), inc=S["sqc"])
                else:
                    kb.do("vector", lambda e, f=f: e.tensor_max(
                        M1[:, :], M1[:, :], habs[:, f % 2, :]),
                        inc=S["sqc"])
                ab_at.append(S["sqc"].v)
                if f == 15:
                    kb.do("vector", lambda e: e.drain(), inc=S["acc"])
            kb.wait("gpsimd", S["acc"])
            kb.do("gpsimd", lambda e: e.tensor_reduce(
                crows[0:1, 4, :], M1[:, :], AXC, OP.max), inc=S["par"])
            kb.wait("vector", S["par"])
            kb.do("vector", lambda e: e.tensor_scalar(
                crows[0:1, 4, :], crows[0:1, 4, :], 1.0 / 127.0, EPS,
                OP.mult, OP.add), inc=S["vrow"])
            kb.wait("tensor", S["vrow"])
            kb.wait("tensor", S["h1"])
            kb.wait("tensor", S["h2q"], evb.get("sbank", 0))
            kb.do("tensor", lambda e: e.matmul(
                ps_c[:, 0, :], onesr[0:1, :], crows[0:1, 4, :],
                start=True, stop=True), inc=S["mmps"])
            kb.wait("vector", S["mmps"])
            kb.do("vector", lambda e, st=st: e.tensor_copy(
                sball[:, st, :], ps_c[:, 0, :]))
            kb.do("vector", lambda e, st=st: e.reciprocal(
                rsb[:, :], sball[:, st, :]))
            for f in range(16):
                kb.do("vector", lambda e, f=f: e.tensor_mul(
                    qt1[:, 0, :], h2buf[:, f, :], rsb[:, :]))
                kb.do("vector", lambda e: e.tensor_scalar(
                    qt1[:, 0, :], qt1[:, 0, :], -128.0, 127.0,
                    OP.max, OP.min))
                kb.do("vector", lambda e: e.tensor_single_scalar(
                    qt1[:, 1, :], qt1[:, 0, :], wscol[:, 14:15], OP.add))
                kb.do("vector", lambda e, f=f, st=st: e.tensor_single_scalar(
                    h2q[:, f, 512 * st:512 * st + 512], qt1[:, 1, :],
                    wscol[:, 14:15], OP.subtract), inc=S["h2q"])
                evb[("hb", f)] = S["h2q"].v
            evb["sbank"] = S["h2q"].v
        h2q_at = S["h2q"].v
        return h2q_at


def _phase_C2(kb, S, sb, pst, st_, h2q, sbg, sbu, sball, rs2b, s2bd, wacc,
              wscol, onesr, identf, gmax, M16, wg_t, wu_t, wd_t, gu_sp,
              h2q_at):
    with ExitStack() as ces:
        wgb = sb(ces, "wgb", [128, 2, 2048], F32)
        wub = sb(ces, "wub", [128, 2, 2048], F32)
        wqg = sb(ces, "wqg", [128, 2048], BF16)
        wqu = sb(ces, "wqu", [128, 2048], BF16)
        g1t = sb(ces, "g1t", [128, 2048], BF16)
        g1u = sb(ces, "g1u", [128, 2048], BF16)
        tmp1 = sb(ces, "tmp1", [128, 2, 512], F32)
        tmp2 = sb(ces, "tmp2", [128, 2, 512], F32)
        gsil = sb(ces, "gsil", [128, 2, 512], F32)
        gust = sb(ces, "gust", [128, 2, 512], F32)
        gabs = sb(ces, "gabs", [128, 2, 512], F32)
        M2 = sb(ces, "M2", [128, 4, 512], F32)
        wdab = sb(ces, "wdab", [128, 2048], F32)
        tcol = sb(ces, "tcol", [128, 1], F32)
        srow = sb(ces, "srow", [1, 512], F32)
        ps_c2 = pst(ces, "ps_c2", [128, 4, 512])
        ps_cx = pst(ces, "ps_cx", [128, 512])
        evb = {}

        kb.do("vector", lambda e: e.memset(M2[:, :, :], 0.0))
        kb.wait("tensor", S["h2q"], h2q_at)

        def plan_ws_final(acol, base, hedged=False):
            kb.wait("vector", S["wabf"])
            w = 32
            while w >= 1:
                kb.do("vector", lambda e, acol=acol, w=w: e.tensor_add(
                    wacc[:, acol:acol + w], wacc[:, acol:acol + w],
                    wacc[:, acol + w:acol + 2 * w]))
                w //= 2
            kb.do("vector", lambda e: e.drain(), inc=S["vrow"])
            kb.wait("tensor", S["vrow"])
            kb.wait("tensor", S["acc"], S["acc"].v)
            kb.do("tensor", lambda e, acol=acol: e.transpose(
                ps_cx[0:1, 0:128], wacc[:, acol:acol + 1], identf[:, :]),
                inc=S["mmps"])
            kb.wait("vector", S["mmps"])
            kb.do("vector", lambda e: e.tensor_copy(
                srow[0:1, 0:128], ps_cx[0:1, 0:128]), inc=S["acc"])
            kb.do("vector", lambda e: e.memset(srow[0:1, 128:256], 0.0))
            k = 64
            while k >= 1:
                x = lambda a, b: srow[0:1, a:b]
                kb.do("vector", lambda e, k=k: e.tensor_add(
                    srow[0:1, 256:256 + k], x(0, k), x(k, 2 * k)))
                kb.do("vector", lambda e, k=k: e.tensor_sub(
                    srow[0:1, 320:320 + k], srow[0:1, 256:256 + k],
                    x(0, k)))
                kb.do("vector", lambda e, k=k: e.tensor_sub(
                    srow[0:1, 384:384 + k], srow[0:1, 256:256 + k],
                    srow[0:1, 320:320 + k]))
                kb.do("vector", lambda e, k=k: e.tensor_sub(
                    srow[0:1, 384:384 + k], x(0, k),
                    srow[0:1, 384:384 + k]))
                kb.do("vector", lambda e, k=k: e.tensor_sub(
                    srow[0:1, 448:448 + k], x(k, 2 * k),
                    srow[0:1, 320:320 + k]))
                kb.do("vector", lambda e, k=k: e.tensor_add(
                    srow[0:1, 384:384 + k], srow[0:1, 384:384 + k],
                    srow[0:1, 448:448 + k]))
                kb.do("vector", lambda e, k=k: e.tensor_add(
                    srow[0:1, 128:128 + k], srow[0:1, 128:128 + k],
                    srow[0:1, 128 + k:128 + 2 * k]))
                kb.do("vector", lambda e, k=k: e.tensor_add(
                    srow[0:1, 128:128 + k], srow[0:1, 128:128 + k],
                    srow[0:1, 384:384 + k]))
                kb.do("vector", lambda e, k=k: e.tensor_copy(
                    x(0, k), srow[0:1, 256:256 + k]))
                k //= 2
            kb.do("vector", lambda e: e.tensor_add(
                srow[0:1, 0:1], srow[0:1, 0:1], srow[0:1, 128:129]))
            nbc = 1
            if hedged:
                nbc = 3
                kb.do("vector", lambda e: e.tensor_scalar(
                    srow[0:1, 0:1], srow[0:1, 0:1], 1.0 / NW_GU, EPS,
                    OP.mult, OP.add))
                kb.do("vector", lambda e: e.tensor_scalar(
                    srow[0:1, 4:5], onesr[0:1, 0:1], 1.0, TWO_M25,
                    OP.mult, OP.add))
                kb.do("vector", lambda e: e.tensor_scalar(
                    srow[0:1, 4:5], srow[0:1, 4:5], 1.0, -1.0,
                    OP.mult, OP.add))
                kb.do("vector", lambda e: e.tensor_scalar(
                    srow[0:1, 4:5], srow[0:1, 4:5], -WBANDC, WBAND,
                    OP.mult, OP.add))
                kb.do("vector", lambda e: e.tensor_scalar(
                    srow[0:1, 1:2], srow[0:1, 4:5], -1.0, 0.5,
                    OP.mult, OP.add))
                kb.do("vector", lambda e: e.tensor_scalar(
                    srow[0:1, 2:3], srow[0:1, 4:5], 1.0, 0.5,
                    OP.mult, OP.add))
                kb.do("vector", lambda e: e.tensor_mul(
                    srow[0:1, 1:2], srow[0:1, 1:2], srow[0:1, 0:1]))
                kb.do("vector", lambda e: e.tensor_mul(
                    srow[0:1, 2:3], srow[0:1, 2:3], srow[0:1, 0:1]),
                    inc=S["vrow"])
            else:
                kb.do("vector", lambda e: e.tensor_scalar(
                    srow[0:1, 0:1], srow[0:1, 0:1], 1.0 / NW_GU, EPS,
                    OP.mult, OP.add), inc=S["vrow"])
            kb.wait("tensor", S["vrow"])
            kb.wait("tensor", S["acc"], S["acc"].v)
            kb.do("tensor", lambda e, nbc=nbc: e.matmul(
                ps_cx[:, 0:nbc], onesr[0:1, :], srow[0:1, 0:nbc],
                start=True, stop=True), inc=S["mmps"])
            kb.wait("vector", S["mmps"])
            if hedged:
                kb.do("vector", lambda e, base=base: e.tensor_copy(
                    wscol[:, base:base + 1], ps_cx[:, 0:1]))
                kb.do("vector", lambda e, base=base: e.tensor_copy(
                    wscol[:, base + 4:base + 5], ps_cx[:, 1:2]))
                kb.do("vector", lambda e, base=base: e.tensor_copy(
                    wscol[:, base + 5:base + 6], ps_cx[:, 2:3]),
                    inc=S["acc"])
            else:
                kb.do("vector", lambda e, base=base: e.tensor_copy(
                    wscol[:, base:base + 1], ps_cx[:, 0:1]), inc=S["acc"])
            kb.do("vector", lambda e, base=base: e.tensor_scalar_mul(
                wscol[:, base + 1:base + 2], wscol[:, base:base + 1], 0.5))
            if not hedged:
                kb.do("vector", lambda e, base=base: e.tensor_scalar_mul(
                    wscol[:, base + 2:base + 3], wscol[:, base:base + 1],
                    -0.5))
            kb.do("vector", lambda e, base=base: e.tensor_scalar_mul(
                wscol[:, base + 3:base + 4], wscol[:, base:base + 1], -1.0),
                inc=S["abs"])

        plan_ws_final(0, 0)
        plan_ws_final(64, 4)
        kb.wait("vector", S["abs"], 2)
        for st in range(4):
            kb.do("vector", lambda e, st=st: e.tensor_single_scalar(
                sbg[:, st, :], sball[:, st, :], wscol[:, 3:4], OP.mult))
            kb.do("vector", lambda e, st=st: e.tensor_single_scalar(
                sbu[:, st, :], sball[:, st, :], wscol[:, 7:8], OP.mult))

        wd_jobs = [("abs", o2, q) for o2 in range(16) for q in range(4)]
        wd_jobs.append(("fin",))
        wd_idx = [0]

        def plan_wd_job():
            if wd_idx[0] >= len(wd_jobs):
                return
            job = wd_jobs[wd_idx[0]]
            wd_idx[0] += 1
            if job[0] == "abs":
                _, o2, q = job
                kb.wait("sync", S["wabf"], S["wabf"].v)
                kb.dma("sync", wdab[:, :],
                       wd_t[o2][:, 16 * q:16 * q + 16, :], S["wab0"])
                kb.wait("scalar", S["wab0"])
                col = 128 + 4 * o2 + q
                kb.do("scalar", lambda e, col=col: e.activation(
                    wdab[:, :], wdab[:, :], AF.Abs,
                    accum_out=wacc[:, col:col + 1]), inc=S["wabf"])
            elif job[0] == "fin":
                plan_ws_final(128, 8, hedged=True)

        for o in range(64):
            os_ = o % 2
            kb.wait("sync", S["q2"], evb.get(("wg", os_), 0))
            kb.wait("sync", S["q2g"], evb.get(("wu", os_), 0))
            kb.dma("sync", wgb[:, os_, :], wg_t[o], S["wg%d" % os_])
            at_wg = S["wg%d" % os_].v
            kb.dma("sync", wub[:, os_, :], wu_t[o], S["wu%d" % os_])
            at_wu = S["wu%d" % os_].v
            plan_wd_job()
            plan_wd_job()
            kb.wait("vector", S["wg%d" % os_], at_wg)
            kb.wait("vector", S["abs"], 2)
            kb.wait("vector", S["gps"], evb.get("wqg_free", 0))
            kb.do("vector", lambda e, os_=os_: e.tensor_single_scalar(
                g1t[:, :], wgb[:, os_, :], wscol[:, 1:2], OP.is_ge))
            kb.do("vector", lambda e, os_=os_: e.scalar_tensor_tensor(
                wqg[:, :], wgb[:, os_, :], wscol[:, 2:3], g1t[:, :],
                OP.is_le, OP.subtract), inc=S["q2"])
            evb[("wg", os_)] = S["q2"].v
            kb.wait("vector", S["wu%d" % os_], at_wu)
            kb.wait("vector", S["ups"], evb.get("wqu_free", 0))
            kb.do("vector", lambda e, os_=os_: e.tensor_single_scalar(
                g1u[:, :], wub[:, os_, :], wscol[:, 5:6], OP.is_ge))
            kb.do("vector", lambda e, os_=os_: e.scalar_tensor_tensor(
                wqu[:, :], wub[:, os_, :], wscol[:, 6:7], g1u[:, :],
                OP.is_le, OP.subtract), inc=S["q2g"])
            evb[("wu", os_)] = S["q2g"].v
            kb.wait("tensor", S["q2"])
            kb.wait("tensor", S["q2g"])
            for st in range(4):
                sp_ = st % 2
                kb.wait("tensor", S["gfree"], evb.get(("pg", sp_), 0))
                for i in range(16):
                    kb.do("tensor", lambda e, i=i, st=st, sp_=sp_: e.matmul(
                        ps_c2[:, sp_, :], wqg[:, 128 * i:128 * i + 128],
                        h2q[:, i, 512 * st:512 * st + 512],
                        start=(i == 0), stop=(i == 15)),
                        inc=(S["gps"] if i == 15 else None))
                if st == 3:
                    evb["wqg_free"] = S["gps"].v
                kb.wait("tensor", S["ufree"], evb.get(("pu", sp_), 0))
                for i in range(16):
                    kb.do("tensor", lambda e, i=i, st=st, sp_=sp_: e.matmul(
                        ps_c2[:, 2 + sp_, :], wqu[:, 128 * i:128 * i + 128],
                        h2q[:, i, 512 * st:512 * st + 512],
                        start=(i == 0), stop=(i == 15)),
                        inc=(S["ups"] if i == 15 else None))
                if st == 3:
                    evb["wqu_free"] = S["ups"].v
                kb.wait("vector", S["gps"])
                kb.wait("vector", S["sil"], evb.get(("t1", sp_), 0))
                kb.do("vector", lambda e, st=st, sp_=sp_: e.tensor_mul(
                    tmp1[:, sp_, :], ps_c2[:, sp_, :], sbg[:, st, :]),
                    inc=S["gfree"])
                evb[("pg", sp_)] = S["gfree"].v
                kb.wait("scalar", S["gfree"])
                kb.wait("scalar", S["guv"], evb.get(("gsil", sp_), 0))
                kb.do("scalar", lambda e, sp_=sp_: e.activation(
                    gsil[:, sp_, :], tmp1[:, sp_, :], AF.Silu),
                    inc=S["sil"])
                evb[("t1", sp_)] = S["sil"].v
                kb.wait("vector", S["ups"])
                kb.do("vector", lambda e, st=st, sp_=sp_: e.tensor_mul(
                    tmp2[:, sp_, :], ps_c2[:, 2 + sp_, :], sbu[:, st, :]),
                    inc=S["ufree"])
                evb[("pu", sp_)] = S["ufree"].v
                kb.wait("vector", S["sil"])
                kb.wait("vector", S["gud%d" % sp_], evb.get(("gs", sp_), 0))
                kb.do("vector", lambda e, sp_=sp_: e.tensor_mul(
                    gust[:, sp_, :], gsil[:, sp_, :], tmp2[:, sp_, :]),
                    inc=S["guv"])
                evb[("gsil", sp_)] = S["guv"].v
                kb.wait("scalar", S["guv"])
                kb.wait("scalar", S["gu"], evb.get(("ga", sp_), 0))
                kb.do("scalar", lambda e, sp_=sp_: e.activation(
                    gabs[:, sp_, :], gust[:, sp_, :], AF.Abs), inc=S["sq"])
                kb.wait("vector", S["sq"])
                kb.do("vector", lambda e, st=st, sp_=sp_: e.tensor_max(
                    M2[:, st, :], M2[:, st, :], gabs[:, sp_, :]))
                if st == 0:
                    kb.do("vector", lambda e, o=o, sp_=sp_: e.reduce_max(
                        gmax[:, o:o + 1], gabs[:, sp_, :], AX),
                        inc=S["gu"])
                else:
                    kb.do("vector", lambda e, sp_=sp_: e.reduce_max(
                        tcol[:, :], gabs[:, sp_, :], AX))
                    kb.do("vector", lambda e, o=o: e.tensor_max(
                        gmax[:, o:o + 1], gmax[:, o:o + 1], tcol[:, :]),
                        inc=S["gu"])
                evb[("ga", sp_)] = S["gu"].v
                kb.wait("sync", S["guv"])
                kb.dma("sync", gu_sp[o][:, 512 * st:512 * st + 512],
                       gust[:, sp_, :], S["gud%d" % sp_])
                evb[("gs", sp_)] = S["gud%d" % sp_].v
        while wd_idx[0] < len(wd_jobs):
            plan_wd_job()
        kb.settle([S["gud0"], S["gud1"]])
        kb.do("vector", lambda e: e.tensor_single_scalar(
            wacc[:, 0:64], gmax[:, :], wscol[:, 8:9], OP.mult))
        kb.do("vector", lambda e: e.tensor_single_scalar(
            wacc[:, 64:128], wacc[:, 0:64], HEDGE_LO, OP.is_ge))
        kb.do("vector", lambda e: e.tensor_single_scalar(
            M16[:, :], wacc[:, 0:64], HEDGE_HI, OP.is_le))
        kb.do("vector", lambda e: e.tensor_mul(
            M16[:, :], M16[:, :], wacc[:, 64:128]), inc=S["gmx"])
        kb.wait("gpsimd", S["gu"])
        kb.wait("vector", S["abs"], 3)
        for st in range(4):
            kb.wait("gpsimd", S["vrow"], S["vrow"].v)
            kb.wait("gpsimd", S["mmps"], S["mmps"].v)
            kb.do("gpsimd", lambda e, st=st: e.tensor_reduce(
                srow[0:1, :], M2[:, st, :], AXC, OP.max), inc=S["par"])
            kb.wait("vector", S["par"])
            kb.do("vector", lambda e: e.tensor_scalar(
                srow[0:1, :], srow[0:1, :], 1.0 / 127.0, EPS,
                OP.mult, OP.add), inc=S["vrow"])
            kb.wait("tensor", S["vrow"])
            kb.wait("tensor", S["acc"], S["acc"].v)
            kb.do("tensor", lambda e: e.matmul(
                ps_cx[:, :], onesr[0:1, :], srow[0:1, :],
                start=True, stop=True), inc=S["mmps"])
            kb.wait("vector", S["mmps"])
            kb.do("vector", lambda e, st=st: e.tensor_copy(
                M2[:, st, :], ps_cx[:, :]), inc=S["acc"])
            kb.do("vector", lambda e, st=st: e.reciprocal(
                rs2b[:, st, :], M2[:, st, :]))
            kb.do("vector", lambda e, st=st: e.tensor_single_scalar(
                s2bd[:, st, :], M2[:, st, :], wscol[:, 11:12], OP.mult),
                inc=S["prep"])
    return S["prep"].v


def _phase_C3(kb, S, sb, pst, st_, rs2b, s2bd, gu_sp, wdq_sp, r1_sp, out_t,
              s2_at, wd_t, wscol, M16):
    with ExitStack() as ces:
        xq2 = sb(ces, "xq2", [128, 64, 1024], BF16)
        wdqb = sb(ces, "wdqb", [128, 8192], BF16)
        gub = sb(ces, "gub", [128, 2, 1024], F32)
        tq = sb(ces, "tq", [128, 1024], F32)
        r1c3 = sb(ces, "r1c3", [128, 512], F32)
        outst = sb(ces, "outst", [128, 512], F32)
        wdw = sb(ces, "wdw", [128, 2, 2048], F32)
        sgw = sb(ces, "sgw", [128, 1024], BF16)
        p0t = sb(ces, "p0t", [128, 1024], BF16)
        p1t = sb(ces, "p1t", [128, 1024], BF16)
        wdq3 = sb(ces, "wdq3", [128, 1024], BF16)
        ps_c3 = pst(ces, "ps_c3", [128, 2, 512])
        evb = {}

        q3_jobs = [(o2, q) for o2 in range(16) for q in range(4)]
        q3_idx = [0]

        def plan_wd_q():
            if q3_idx[0] >= len(q3_jobs):
                return
            o2, q = q3_jobs[q3_idx[0]]
            q3_idx[0] += 1
            wsl = q3_idx[0] % 2
            kb.wait("sync", S["wqv"], evb.get(("wfree", wsl), 0))
            kb.dma("sync", wdw[:, wsl, :],
                   wd_t[o2][:, 16 * q:16 * q + 16, :], S["wdb%d" % wsl])
            for h in range(2):
                hs = slice(1024 * h, 1024 * h + 1024)
                kb.wait("scalar", S["wdb%d" % wsl])
                kb.wait("scalar", S["wqv"], evb.get("sgread", 0))
                kb.do("scalar", lambda e, hs=hs, wsl=wsl: e.activation(
                    sgw[:, :], wdw[:, wsl, hs], AF.Sign, scale=-1.0))
                kb.do("scalar", lambda e, hs=hs, wsl=wsl: e.activation(
                    wdw[:, wsl, hs], wdw[:, wsl, hs], AF.Abs),
                    inc=S["wqs"])
                kb.wait("vector", S["wqs"])
                kb.wait("vector", S["gmx"])
                kb.wait("vector", S["wdqd"], evb.get("q3spill", 0))
                kb.do("vector", lambda e, hs=hs, wsl=wsl:
                      e.tensor_single_scalar(
                          p0t[:, :], wdw[:, wsl, hs], wscol[:, 9:10],
                          OP.is_ge))
                kb.do("vector", lambda e, hs=hs, wsl=wsl:
                      e.tensor_single_scalar(
                          p1t[:, :], wdw[:, wsl, hs], wscol[:, 12:13],
                          OP.is_ge))
                kb.do("vector", lambda e, hs=hs, wsl=wsl:
                      e.tensor_single_scalar(
                          wdq3[:, :], wdw[:, wsl, hs], wscol[:, 13:14],
                          OP.is_ge), inc=S["wqv"])
                evb[("wfree", wsl)] = S["wqv"].v
                kb.do("vector", lambda e: e.tensor_add(
                    p1t[:, :], p1t[:, :], wdq3[:, :]))
                kb.do("vector", lambda e: e.scalar_tensor_tensor(
                    p1t[:, :], p1t[:, :], 0.5, p0t[:, :],
                    OP.mult, OP.subtract))
                for c in range(8):
                    co = 16 * q + 8 * h + c
                    kb.do("vector", lambda e, c=c, co=co:
                          e.tensor_single_scalar(
                              p1t[:, 128 * c:128 * c + 128],
                              p1t[:, 128 * c:128 * c + 128],
                              M16[:, co:co + 1], OP.mult))
                kb.do("vector", lambda e: e.tensor_add(
                    p0t[:, :], p0t[:, :], p1t[:, :]))
                kb.do("vector", lambda e: e.tensor_mul(
                    wdq3[:, :], p0t[:, :], sgw[:, :]), inc=S["wqv"])
                evb["sgread"] = S["wqv"].v
                kb.wait("sync", S["wqv"])
                kb.dma("sync",
                       wdq_sp[o2][:, 2048 * q + 1024 * h:
                                  2048 * q + 1024 * h + 1024],
                       wdq3[:, :], S["wdqd"])
                evb["q3spill"] = S["wdqd"].v
            if q == 3:
                evb[("wdone", o2)] = S["wdqd"].v

        kb.wait("vector", S["prep"], s2_at)
        for st2 in range(2):
            kb.wait("vector", S["dps"], S["dps"].v)
            for i2 in range(64):
                if st2 == 0 and i2 % 8 == 0:
                    plan_wd_q()
                isl = i2 % 2
                kb.wait("sync", S["xq2"], evb.get(("gb", isl), 0))
                kb.dma("sync", gub[:, isl, :],
                       gu_sp[i2][:, 1024 * st2:1024 * st2 + 1024],
                       S["gurd%d" % isl])
                kb.wait("vector", S["gurd%d" % isl])
                kb.do("vector", lambda e, isl=isl, st2=st2: e.tensor_mul(
                    tq[:, 0:512], gub[:, isl, 0:512], rs2b[:, 2 * st2, :]))
                kb.do("vector", lambda e, isl=isl, st2=st2: e.tensor_mul(
                    tq[:, 512:1024], gub[:, isl, 512:1024],
                    rs2b[:, 2 * st2 + 1, :]))
                kb.do("vector", lambda e: e.tensor_scalar(
                    tq[:, :], tq[:, :], -128.0, 127.0, OP.max, OP.min))
                kb.do("vector", lambda e: e.tensor_single_scalar(
                    tq[:, :], tq[:, :], wscol[:, 14:15], OP.add))
                kb.do("vector", lambda e, i2=i2: e.tensor_single_scalar(
                    xq2[:, i2, :], tq[:, :], wscol[:, 14:15], OP.subtract),
                    inc=S["xq2"])
                evb[("gb", isl)] = S["xq2"].v
            xq_done = S["xq2"].v
            for o2 in range(16):
                if st2 == 0:
                    while q3_idx[0] < min(len(q3_jobs), 4 * (o2 + 3)):
                        plan_wd_q()
                kb.wait("sync", S["dps"], evb.get("wd", 0))
                kb.wait("sync", S["wdqd"], evb.get(("wdone", o2), 0))
                kb.dma("sync", wdqb[:, :], wdq_sp[o2], S["wdl0"])
                kb.wait("tensor", S["wdl0"])
                kb.wait("tensor", S["xq2"], xq_done)
                for sub in range(2):
                    kb.wait("tensor", S["dfree"], evb.get(("pd", sub), 0))
                    for i2 in range(64):
                        kb.do("tensor", lambda e, i2=i2, sub=sub:
                              e.matmul(ps_c3[:, sub, :],
                                       wdqb[:, 128 * i2:128 * i2 + 128],
                                       xq2[:, i2, 512 * sub:512 * sub + 512],
                                       start=(i2 == 0), stop=(i2 == 63)),
                              inc=(S["dps"] if i2 == 63 else None))
                    if sub == 1:
                        evb["wd"] = S["dps"].v
                    kb.wait("sync", S["oev"], evb.get("rx", 0))
                    kb.dma("sync", r1c3[:, :],
                           r1_sp[o2][:, 1024 * st2 + 512 * sub:
                                     1024 * st2 + 512 * sub + 512],
                           S["rc0"])
                    kb.wait("vector", S["dps"])
                    kb.wait("vector", S["rc0"])
                    kb.wait("vector", S["ow0"], evb.get("os", 0))
                    kb.do("vector", lambda e, st2=st2, sub=sub: e.tensor_mul(
                        outst[:, :], ps_c3[:, sub, :],
                        s2bd[:, 2 * st2 + sub, :]), inc=S["dfree"])
                    evb[("pd", sub)] = S["dfree"].v
                    kb.do("vector", lambda e: e.tensor_add(
                        outst[:, :], outst[:, :], r1c3[:, :]), inc=S["oev"])
                    evb["rx"] = S["oev"].v
                    kb.wait("sync", S["oev"])
                    kb.dma("sync",
                           out_t[o2][:, 1024 * st2 + 512 * sub:
                                     1024 * st2 + 512 * sub + 512],
                           outst[:, :], S["ow0"])
                    evb["os"] = S["ow0"].v
        kb.wait("sync", S["ow0"])


def _emit(nc, kb):
    with nc.Block() as block:
        @block.sync
        def _(e):
            for f in kb.plan["sync"]:
                f(e)

        @block.gpsimd
        def _(e):
            for f in kb.plan["gpsimd"]:
                f(e)

        @block.tensor
        def _(e):
            for f in kb.plan["tensor"]:
                f(e)

        @block.vector
        def _(e):
            for f in kb.plan["vector"]:
                f(e)

        @block.scalar
        def _(e):
            for f in kb.plan["scalar"]:
                f(e)



def _tile_attn_w(w):
    return np.ascontiguousarray(
        w.reshape(16, 128, 16, 128).transpose(2, 1, 0, 3))


def _prep_core(xT, parity):
    own = [2 * t + parity for t in range(16)]
    cols = np.concatenate([np.arange(128 * j, 128 * j + 128) for j in own])
    xTo = xT[:, cols]
    x_own_t = np.ascontiguousarray(
        xTo.reshape(16, 128, 8, 256).transpose(2, 1, 0, 3))
    x_full_t = np.ascontiguousarray(
        xT.reshape(16, 128, 16, 256).transpose(2, 1, 0, 3))
    deltas = (128 * parity, 128 * parity + 256)
    m = np.zeros((2, 128, 512), dtype=np.float32)
    tt = np.arange(128)[:, None]
    cc = np.arange(512)[None, :]
    for k, dl in enumerate(deltas):
        m[k] = (cc <= tt + dl).astype(np.float32)
    m = np.ascontiguousarray(
        m.transpose(1, 0, 2)).astype(ml_dtypes.bfloat16)
    return x_own_t, x_full_t, m, cols


def kernel(L0_attn_x, L0_attn_Wq, L0_attn_Wk, L0_attn_Wv, L0_attn_Wo,
           L0_ffn_w_gate, L0_ffn_w_up, L0_ffn_w_down):
    x = np.ascontiguousarray(np.asarray(L0_attn_x, dtype=np.float32))
    wq = _tile_attn_w(np.asarray(L0_attn_Wq, dtype=np.float32))
    wk = _tile_attn_w(np.asarray(L0_attn_Wk, dtype=np.float32))
    wv = _tile_attn_w(np.asarray(L0_attn_Wv, dtype=np.float32))
    wo = _tile_attn_w(np.asarray(L0_attn_Wo, dtype=np.float32))
    wg = np.ascontiguousarray(np.asarray(L0_ffn_w_gate, dtype=np.float32)
                              .reshape(64, 128, 16, 128)
                              .transpose(0, 3, 2, 1))
    wu = np.ascontiguousarray(np.asarray(L0_ffn_w_up, dtype=np.float32)
                              .reshape(64, 128, 16, 128)
                              .transpose(0, 3, 2, 1))
    wd = np.ascontiguousarray(np.asarray(L0_ffn_w_down, dtype=np.float32)
                              .reshape(16, 128, 64, 128)
                              .transpose(0, 3, 2, 1))
    ident = np.eye(128, dtype=ml_dtypes.bfloat16)
    identf = np.eye(128, dtype=np.float32)
    ones = np.ones((128, 1), dtype=ml_dtypes.bfloat16)
    onesr = np.ones((1, 128), dtype=np.float32)

    nc = build_nc(7)
    in_maps = []
    core_cols = []
    for c in range(8):
        b, parity = c // 2, c % 2
        xT = np.ascontiguousarray(x[b].T)
        x_own_t, x_full_t, m, colsv = _prep_core(xT, parity)
        core_cols.append(colsv)
        in_maps.append({
            "x_own_t": x_own_t, "x_full_t": x_full_t,
            "wq_t": wq, "wk_t": wk, "wv_t": wv, "wo_t": wo,
            "wg_t": wg, "wu_t": wu, "wd_t": wd,
            "masks": m, "identc": ident, "identfc": identf,
            "onesc": ones, "onesrc": onesr,
        })
    res = run_bass_kernel_spmd(nc, in_maps, core_ids=list(range(8)))
    y = np.empty((4, 4096, 2048), dtype=np.float32)
    for c in range(8):
        b = c // 2
        o = np.asarray(res.results[c]["out_t"])
        y[b][core_cols[c], :] = o.reshape(2048, 2048).T
    return y


# version 11 (final)
# speedup vs baseline: 1.1212x; 1.1212x over previous
import numpy as np
import ml_dtypes
from contextlib import ExitStack

import concourse.bass as bass
import concourse.mybir as mybir
from concourse.bass_utils import run_bass_kernel_spmd

F32 = mybir.dt.float32
BF16 = mybir.dt.bfloat16
I32 = mybir.dt.int32
AF = mybir.ActivationFunctionType
OP = mybir.AluOpType
AX = mybir.AxisListType.X
AXC = mybir.AxisListType.C

D = 2048
T = 4096
EPS = 1e-5
NW_GU = 8192 * 2048
ISQ = 1.0 / 64.0
TWO_M25 = 2.0 ** -25
WBAND = 5e-7
WBANDC = WBAND * (1 << 25)
MAG32 = 12582912.0
MAGK = 1.5 * float((1 << 52) - (1 << 23)) * float(1 << 25)
HEDGE_LO = 0.14
HEDGE_HI = 0.30

ENGS = ("sync", "gpsimd", "tensor", "vector", "scalar")


class Sem:
    __slots__ = ("h", "v", "name")


class KB:

    def __init__(self, nc, es):
        self.nc = nc
        self.es = es
        self.plan = {e: [] for e in ENGS}

    def sem(self, name):
        s = Sem()
        s.h = self.es.enter_context(self.nc.semaphore(name))
        s.v = 0
        s.name = name
        return s

    def wait(self, eng, sem, val=None):
        v = sem.v if val is None else val
        if v <= 0:
            return
        self.plan[eng].append(lambda e, h=sem.h, vv=v: e.wait_ge(h, vv))

    def do(self, eng, fn, inc=None, amt=1):
        dr = eng in ("vector", "scalar", "gpsimd")
        if inc is None:
            if dr:
                self.plan[eng].append(lambda e, f=fn: (f(e), e.drain()))
            else:
                self.plan[eng].append(fn)
        else:
            inc.v += amt
            if dr:
                self.plan[eng].append(
                    lambda e, f=fn, h=inc.h, a=amt:
                    (f(e).then_inc(h, a), e.drain()))
            else:
                self.plan[eng].append(
                    lambda e, f=fn, h=inc.h, a=amt: f(e).then_inc(h, a))

    def dma(self, eng, dst, src, sem):
        self.do(eng, lambda e, d=dst, s=src: e.dma_start(out=d, in_=s),
                inc=sem, amt=16)

    def settle(self, sems):
        for s in sems:
            self.wait("sync", s)

    def fence(self, sem, engs=ENGS):
        for p in engs:
            self.do(p, lambda e: e.drain(), inc=sem)
        v = sem.v
        for c in engs:
            self.wait(c, sem, v)


def build_nc(n_phases=7):
    nc = bass.Bass("TRN2", target_bir_lowering=False)

    x_own_t = nc.dram_tensor("x_own_t", [8, 128, 16, 256], F32,
                             kind="ExternalInput").ap()
    x_full_t = nc.dram_tensor("x_full_t", [16, 128, 16, 256], F32,
                              kind="ExternalInput").ap()
    wq_t = nc.dram_tensor("wq_t", [16, 128, 16, 128], F32,
                          kind="ExternalInput").ap()
    wk_t = nc.dram_tensor("wk_t", [16, 128, 16, 128], F32,
                          kind="ExternalInput").ap()
    wv_t = nc.dram_tensor("wv_t", [16, 128, 16, 128], F32,
                          kind="ExternalInput").ap()
    wo_t = nc.dram_tensor("wo_t", [16, 128, 16, 128], F32,
                          kind="ExternalInput").ap()
    wg_t = nc.dram_tensor("wg_t", [64, 128, 16, 128], F32,
                          kind="ExternalInput").ap()
    wu_t = nc.dram_tensor("wu_t", [64, 128, 16, 128], F32,
                          kind="ExternalInput").ap()
    wd_t = nc.dram_tensor("wd_t", [16, 128, 64, 128], F32,
                          kind="ExternalInput").ap()
    masks = nc.dram_tensor("masks", [128, 2, 512], BF16,
                           kind="ExternalInput").ap()
    identc = nc.dram_tensor("identc", [128, 128], BF16,
                            kind="ExternalInput").ap()
    identfc = nc.dram_tensor("identfc", [128, 128], F32,
                             kind="ExternalInput").ap()
    onesc = nc.dram_tensor("onesc", [128, 1], BF16,
                           kind="ExternalInput").ap()
    onesrc = nc.dram_tensor("onesrc", [1, 128], F32,
                            kind="ExternalInput").ap()
    out_t = nc.dram_tensor("out_t", [16, 128, 2048], F32,
                           kind="ExternalOutput").ap()

    q_sp = nc.dram_tensor("q_sp", [16, 128, 2048], BF16, kind="Internal").ap()
    k_sp = nc.dram_tensor("k_sp", [16, 128, 4096], BF16, kind="Internal").ap()
    v_sp = nc.dram_tensor("v_sp", [16, 128, 4096], BF16, kind="Internal").ap()
    r1_sp = nc.dram_tensor("r1_sp", [16, 128, 2048], F32, kind="Internal").ap()
    gu_sp = nc.dram_tensor("gu_sp", [64, 128, 2048], F32, kind="Internal").ap()
    wdq_sp = nc.dram_tensor("wdq_sp", [16, 128, 8192], BF16,
                            kind="Internal").ap()

    with ExitStack() as es:
        kb = KB(nc, es)
        sb = lambda st, name, shape, dt: st.enter_context(
            nc.sbuf_tensor(name, shape, dt))
        pst = lambda st, name, shape, dt=F32: st.enter_context(
            nc.psum_tensor(name, shape, dt))

        identb = sb(es, "identb", [128, 128], BF16)
        identf = sb(es, "identf", [128, 128], F32)
        onesb = sb(es, "onesb", [128, 1], BF16)
        onesr = sb(es, "onesr", [1, 128], F32)
        cols = sb(es, "cols", [128, 32], F32)
        bcols = sb(es, "bcols", [128, 8], BF16)
        wacc = sb(es, "wacc", [128, 192], F32)
        wscol = sb(es, "wscol", [128, 16], F32)
        gmax = sb(es, "gmax", [128, 64], F32)
        M16 = sb(es, "M16", [128, 64], F32)

        S = {}
        for n in ("const w sq sqc tst vrow u sg h1 mmps ev wbf bfree "
                  "vt vtev sps sevS sevV mx exp vn ct ctev pt ptev ptd pv "
                  "o2 g wabf abs r1 prep h2q q2 q2g par acc "
                  "gps ups gfree ufree sil gu guv wdq wdqd xq2 "
                  "gmx wdb0 wdb1 wqs wqv "
                  "dps dfree oev f "
                  "x0 x1 w0 w1 k0 k1 ql0 ql1 vv dq0 dq1 dkv0 dkv1 "
                  "wab0 wab1 xc0 xc1 r1d0 r1d1 rr0 rr1 wg0 wg1 wu0 wu1 "
                  "gud0 gud1 gurd0 gurd1 wdl0 wdl1 rc0 ow0").split():
            S[n] = kb.sem(n)

        kb.dma("sync", identb[:, :], identc[:, :], S["const"])
        kb.dma("sync", identf[:, :], identfc[:, :], S["const"])
        kb.dma("sync", onesb[:, :], onesc[:, :], S["const"])
        kb.dma("sync", onesr[0:1, :], onesrc[:, :], S["const"])
        for e in ("tensor", "vector"):
            kb.wait(e, S["const"])

        def plan_ln(get_chunk, W, wr_fn, scr, rows, ps_sx, ps_sq,
                    ps_mb, ps_rb, evb, key):
            xb = [scr["xb"][:, 0, :], scr["xb"][:, 1, :]]
            sqb = [scr["sqb"][:, 0, :], scr["sqb"][:, 1, :]]
            t1 = scr["t1"]
            m_row, t_row = rows[0:1, 0, :], rows[0:1, 1, :]
            s_row, r_row = rows[0:1, 2, :], rows[0:1, 3, :]
            cast_at = []
            for i in range(16):
                if i >= 2:
                    kb.wait("scalar", S["tst"], cast_at[i - 2])
                kb.do("scalar", lambda e, i=i, b=xb[i % 2]: e.copy(
                    b, get_chunk(i)))
                kb.do("scalar", lambda e, i=i, b=sqb[i % 2]: e.square(
                    b, get_chunk(i)), inc=S["sq"])
                kb.wait("tensor", S["sq"])
                if i == 0:
                    kb.wait("tensor", S["vrow"], evb.get((key, "rows"), 0))
                kb.do("tensor", lambda e, i=i, b=xb[i % 2]: e.matmul(
                    ps_sx, onesb[:, :], b, start=(i == 0), stop=(i == 15)))
                kb.do("tensor", lambda e, i=i, b=sqb[i % 2]: e.matmul(
                    ps_sq, onesb[:, :], b, start=(i == 0), stop=(i == 15)),
                    inc=S["tst"])
                cast_at.append(S["tst"].v)
            kb.do("tensor", lambda e: e.drain(), inc=S["mmps"])
            kb.wait("vector", S["mmps"])
            kb.do("vector", lambda e: e.tensor_scalar_mul(
                m_row, ps_sx, 1.0 / D))
            kb.do("vector", lambda e: e.tensor_scalar_mul(
                t_row, ps_sq, 1.0 / D))
            kb.do("vector", lambda e: e.tensor_mul(s_row, m_row, m_row))
            kb.do("vector", lambda e: e.tensor_sub(t_row, t_row, s_row))
            kb.do("vector", lambda e: e.tensor_scalar_add(
                t_row, t_row, EPS), inc=S["u"])
            kb.wait("scalar", S["u"])
            kb.do("scalar", lambda e: e.sqrt(s_row, t_row), inc=S["sg"])
            kb.wait("vector", S["sg"])
            kb.do("vector", lambda e: e.reciprocal(r_row, s_row))
            kb.do("vector", lambda e: e.tensor_mul(s_row, r_row, r_row))
            kb.do("vector", lambda e: e.tensor_mul(s_row, s_row, t_row))
            kb.do("vector", lambda e: e.tensor_scalar(
                s_row, s_row, -0.5, 1.5, OP.mult, OP.add))
            kb.do("vector", lambda e: e.tensor_mul(r_row, r_row, s_row),
                  inc=S["vrow"])
            evb[(key, "rows")] = S["vrow"].v
            kb.wait("tensor", S["vrow"])
            kb.wait("tensor", S["h1"], evb.get((key, "bc"), 0))
            kb.do("tensor", lambda e: e.matmul(
                ps_mb, onesr[0:1, :], m_row, start=True, stop=True))
            kb.do("tensor", lambda e: e.matmul(
                ps_rb, onesr[0:1, :], r_row, start=True, stop=True),
                inc=S["mmps"])
            kb.wait("vector", S["mmps"])
            for i in range(16):
                kb.do("vector", lambda e, i=i: e.tensor_sub(
                    t1, get_chunk(i), ps_mb))
                kb.do("vector", lambda e, i=i: wr_fn(e, i, t1, ps_rb),
                      inc=(S["h1"] if i == 15 else None))
            evb[(key, "bc")] = S["h1"].v

        with ExitStack() as aes:
            xbuf = sb(aes, "xbuf", [128, 2, 16, 256], F32)
            wbuf = sb(aes, "wbuf", [128, 2, 2048], BF16)
            ascr = {
                "xb": sb(aes, "a_xb", [128, 2, 256], BF16),
                "sqb": sb(aes, "a_sqb", [128, 2, 256], BF16),
                "t1": sb(aes, "a_t1", [128, 256], F32)[:, :],
            }
            arows = sb(aes, "a_rows", [1, 4, 256], F32)
            ps_a = pst(aes, "ps_a", [128, 8, 512])
            evb = {}

            def ln_banks():
                return (ps_a[0:1, 6, 0:256], ps_a[0:1, 7, 0:256],
                        ps_a[:, 4, 0:256], ps_a[:, 5, 0:256])

            with ExitStack() as a1es:
                h1o = sb(a1es, "h1o", [128, 8, 16, 256], BF16)
                qstage = sb(a1es, "qstage", [128, 2, 2048], BF16)

                for st in range(8):
                    slot = st % 2
                    sx = S["x%d" % slot]
                    kb.wait("sync", S["h1"], max(0, S["h1"].v - 1))
                    kb.dma("sync", xbuf[:, slot, :, :], x_own_t[st], sx)
                    kb.wait("vector", sx)
                    kb.wait("scalar", sx)
                    gc = lambda i, slot=slot: xbuf[:, slot, i, :]

                    def wr_own(e, i, t1, rps, st=st):
                        return e.tensor_mul(h1o[:, st, i, :], t1, rps)
                    p_sx, p_sq, p_mb, p_rb = ln_banks()
                    plan_ln(gc, 256, wr_own, ascr, arows,
                            p_sx, p_sq, p_mb, p_rb, evb, "ln")
                h1_own = S["h1"].v

                for f in range(16):
                    fs = f % 2
                    sw = S["w%d" % fs]
                    kb.wait("gpsimd", S["wbf"], evb.get(("wb", fs), 0))
                    kb.do("gpsimd", lambda e, f=f, fs=fs: e.dma_start(
                        out=wbuf[:, fs, :], in_=wq_t[f]), inc=sw, amt=16)
                    kb.wait("tensor", sw)
                    kb.wait("tensor", S["h1"], h1_own)
                    for st8 in range(8):
                        b = st8 % 4
                        kb.wait("tensor", S["ev"], evb.get(("a", b), 0))
                        for i in range(16):
                            kb.do("tensor",
                                  lambda e, fs=fs, st8=st8, i=i, b=b:
                                  e.matmul(ps_a[:, b, 0:256],
                                           wbuf[:, fs, 128 * i:128 * i + 128],
                                           h1o[:, st8, i, :],
                                           start=(i == 0), stop=(i == 15)),
                                  inc=(S["mmps"] if i == 15 else None))
                        kb.wait("vector", S["mmps"])
                        kb.wait("vector", S["dq%d" % fs],
                                evb.get(("q", fs), 0))
                        kb.do("vector", lambda e, fs=fs, st8=st8, b=b:
                              e.tensor_copy(
                                  qstage[:, fs, 256 * st8:256 * st8 + 256],
                                  ps_a[:, b, 0:256]), inc=S["ev"])
                        evb[("a", b)] = S["ev"].v
                    kb.do("tensor", lambda e: e.drain(), inc=S["wbf"])
                    evb[("wb", fs)] = S["wbf"].v
                    kb.wait("sync", S["ev"])
                    kb.dma("sync", q_sp[f], qstage[:, fs, :], S["dq%d" % fs])
                    evb[("q", fs)] = S["dq%d" % fs].v

            if n_phases >= 2:
                kb.settle([S["dq0"], S["dq1"]])
                kb.fence(S["f"])
                with ExitStack() as a2es:
                    H1 = sb(a2es, "H1", [128, 16, 4096], BF16)
                    kvstage = sb(a2es, "kvstage", [128, 2, 4096], BF16)

                    for st in range(16):
                        slot = st % 2
                        sx = S["x%d" % slot]
                        kb.wait("sync", S["h1"], max(0, S["h1"].v - 1))
                        kb.dma("sync", xbuf[:, slot, :, :], x_full_t[st], sx)
                        kb.wait("vector", sx)
                        kb.wait("scalar", sx)
                        gc = lambda i, slot=slot: xbuf[:, slot, i, :]

                        def wr_full(e, i, t1, rps, st=st):
                            return e.tensor_mul(
                                H1[:, i, 256 * st:256 * st + 256], t1, rps)
                        p_sx, p_sq, p_mb, p_rb = ln_banks()
                        plan_ln(gc, 256, wr_full, ascr, arows,
                                p_sx, p_sq, p_mb, p_rb, evb, "ln")
                    h1_full = S["h1"].v

                    for wt, sp in ((wk_t, k_sp), (wv_t, v_sp)):
                        for f in range(16):
                            fs = f % 2
                            sw = S["w%d" % fs]
                            kb.wait("gpsimd", S["wbf"],
                                    evb.get(("wb", fs), 0))
                            kb.do("gpsimd", lambda e, wt=wt, f=f, fs=fs:
                                  e.dma_start(out=wbuf[:, fs, :], in_=wt[f]),
                                  inc=sw, amt=16)
                            kb.wait("tensor", sw)
                            kb.wait("tensor", S["h1"], h1_full)
                            for g in range(2):
                                for strip in range(4):
                                    kb.wait("tensor", S["ev"],
                                            evb.get(("a", 4 * g + strip), 0))
                                for i in range(16):
                                    for strip in range(4):
                                        col = 2048 * g + 512 * strip
                                        kb.do("tensor",
                                              lambda e, fs=fs, i=i, g=g,
                                              strip=strip, col=col:
                                              e.matmul(
                                                  ps_a[:, 4 * g + strip, :],
                                                  wbuf[:, fs,
                                                       128 * i:128 * i + 128],
                                                  H1[:, i, col:col + 512],
                                                  start=(i == 0),
                                                  stop=(i == 15)),
                                              inc=(S["mmps"]
                                                   if (i == 15 and strip == 3)
                                                   else None))
                                kb.wait("vector", S["mmps"])
                                kb.wait("vector", S["dkv%d" % fs],
                                        evb.get(("kv", fs), 0))
                                for strip in range(4):
                                    col = 2048 * g + 512 * strip
                                    kb.do("vector",
                                          lambda e, fs=fs, col=col, g=g,
                                          strip=strip:
                                          e.tensor_copy(
                                              kvstage[:, fs, col:col + 512],
                                              ps_a[:, 4 * g + strip, :]),
                                          inc=S["ev"])
                                    evb[("a", 4 * g + strip)] = S["ev"].v
                            kb.do("tensor", lambda e: e.drain(),
                                  inc=S["wbf"])
                            evb[("wb", fs)] = S["wbf"].v
                            kb.wait("sync", S["ev"])
                            kb.dma("sync", sp[f], kvstage[:, fs, :],
                                   S["dkv%d" % fs])
                            evb[("kv", fs)] = S["dkv%d" % fs].v

        if n_phases >= 3:
            kb.settle([S["dkv0"], S["dkv1"]])
            kb.fence(S["f"])
            with ExitStack() as o2s:
                out2 = sb(o2s, "out2", [128, 16, 2048], BF16)
                _phase_B(kb, S, sb, pst, o2s, out2, masks, identb, onesb,
                         cols, bcols, wacc, q_sp, k_sp, v_sp, wg_t, wu_t)
                if n_phases >= 4:
                    kb.fence(S["f"])
                    _phase_C1a(kb, S, sb, pst, o2s, out2, wo_t, x_own_t,
                               r1_sp)
            if n_phases >= 5:
                kb.fence(S["f"])
                with ExitStack() as rs:
                    rs2b = sb(rs, "rs2b", [128, 4, 512], F32)
                    s2bd = sb(rs, "s2bd", [128, 4, 512], F32)
                    with ExitStack() as hs:
                        h2q = sb(hs, "h2q", [128, 16, 2048], BF16)
                        sbg = sb(hs, "sbg", [128, 4, 512], F32)
                        sbu = sb(hs, "sbu", [128, 4, 512], F32)
                        sball = sb(hs, "sball", [128, 4, 512], F32)
                        h2q_at = _phase_C1b(
                            kb, S, sb, pst, hs, h2q, sball, wscol,
                            onesb, onesr, r1_sp, plan_ln)
                        if n_phases >= 6:
                            kb.fence(S["f"])
                            s2_at = _phase_C2(
                                kb, S, sb, pst, hs, h2q, sbg, sbu, sball,
                                rs2b, s2bd, wacc, wscol, onesr, identf,
                                gmax, M16, wg_t, wu_t, wd_t, gu_sp,
                                h2q_at)
                    if n_phases >= 7:
                        kb.fence(S["f"])
                        _phase_C3(kb, S, sb, pst, rs, rs2b, s2bd,
                                  gu_sp, wdq_sp, r1_sp, out_t, s2_at,
                                  wd_t, wscol, M16)

        _emit(nc, kb)
    return nc


def _phase_B(kb, S, sb, pst, st_, out2, masks, identb, onesb, cols, bcols,
             wacc, q_sp, k_sp, v_sp, wg_t, wu_t):
    with ExitStack() as bes:
        maskb = sb(bes, "maskb", [128, 2, 512], BF16)
        trow = sb(bes, "trow", [1, 1024], F32)
        g32 = sb(bes, "g32", [1, 1024], F32)
        gbf = sb(bes, "gbf", [1, 1024], BF16)
        crow = sb(bes, "crow", [1, 512], BF16)
        kbuf = sb(bes, "kbuf", [128, 2, 4096], BF16)
        vbuf = sb(bes, "vbuf", [128, 4096], BF16)
        qbuf = sb(bes, "qbuf", [128, 2, 2048], BF16)
        vtm = sb(bes, "vtm", [128, 32, 128], BF16)
        Ssb = sb(bes, "Ssb", [128, 2, 512], F32)
        Psb = sb(bes, "Psb", [128, 2, 4096], BF16)
        PT = sb(bes, "PT", [128, 32, 512], BF16)
        wab = sb(bes, "wab", [128, 2, 2048], F32)
        ps_b = pst(bes, "ps_b", [128, 4, 512])
        ps_bt = pst(bes, "ps_bt", [128, 2, 1024], BF16)
        ps_sm = pst(bes, "ps_sm", [1, 1024])

        BK_S, BK_T, BK_O = (0, 1), (0, 1), (2, 3)

        bfree_at = [0]
        loads_at = {}
        evb = {}

        kb.dma("sync", maskb[:, :, :], masks[:, :, :], S["const"])
        kb.wait("vector", S["const"])

        abs_jobs = [("g", o) for o in range(64)] + \
                   [("u", o) for o in range(64)]
        abs_idx = [0]
        abs_done = []

        def plan_absum_slice(n=8):
            for _ in range(n):
                if abs_idx[0] >= len(abs_jobs):
                    return
                mat, o = abs_jobs[abs_idx[0]]
                abs_idx[0] += 1
                slot = abs_idx[0] % 2
                if len(abs_done) >= 2:
                    kb.wait("sync", S["wabf"], abs_done[-2])
                srcw = wg_t[o] if mat == "g" else wu_t[o]
                kb.dma("sync", wab[:, slot, :], srcw, S["wab%d" % slot])
                kb.wait("scalar", S["wab%d" % slot])
                col = o if mat == "g" else 64 + o
                kb.do("scalar", lambda e, slot=slot, col=col: e.activation(
                    wab[:, slot, :], wab[:, slot, :], AF.Abs,
                    accum_out=wacc[:, col:col + 1]), inc=S["wabf"])
                abs_done.append(S["wabf"].v)

        def plan_kq_loads(h):
            kb.wait("sync", S["bfree"], bfree_at[0])
            kb.dma("sync", kbuf[:, h % 2, :], k_sp[h], S["k%d" % (h % 2)])
            kb.dma("sync", qbuf[:, h % 2, :], q_sp[h], S["ql%d" % (h % 2)])
            loads_at[("kq", h)] = (S["k%d" % (h % 2)].v,
                                   S["ql%d" % (h % 2)].v)

        def plan_v_load(h):
            kb.wait("sync", S["vtev"], evb.get("vtm_done", 0))
            kb.dma("sync", vbuf[:, :], v_sp[h], S["vv"])
            loads_at[("v", h)] = S["vv"].v

        plan_kq_loads(0)
        plan_v_load(0)
        for h in range(16):
            hs = h % 2
            if h + 1 < 16:
                plan_kq_loads(h + 1)

            kv, qv = loads_at[("kq", h)]
            kb.wait("tensor", S["k%d" % hs], kv)
            kb.wait("tensor", S["ql%d" % hs], qv)
            kb.wait("tensor", S["vv"], loads_at[("v", h)])
            for g in range(8):
                bk = BK_T[g % 2]
                kb.wait("tensor", S["vtev"], evb.get(("vt", bk), 0))
                kb.wait("tensor", S["ptev"], evb.get(("pt", bk), 0))
                for kk in range(4):
                    kb.do("tensor", lambda e, g=g, kk=kk, bk=bk: e.transpose(
                        ps_bt[:, bk, 128 * kk:128 * kk + 128],
                        vbuf[:, 512 * g + 128 * kk:512 * g + 128 * kk + 128],
                        identb[:, :]), inc=(S["vt"] if kk == 3 else None))
                kb.wait("vector", S["vt"])
                kb.do("vector", lambda e, g=g, bk=bk: e.tensor_copy(
                    vtm[:, 4 * g:4 * g + 4, :], ps_bt[:, bk, 0:512]),
                    inc=S["vtev"])
                evb[("vt", bk)] = S["vtev"].v
            evb["vtm_done"] = S["vtev"].v
            if h + 1 < 16:
                plan_v_load(h + 1)

            kb.wait("tensor", S["vtev"], evb["vtm_done"])
            kb.wait("tensor", S["vtev"], evb.get("sm_read", 0))
            for c5 in range(8):
                for kk in range(4):
                    kb.do("tensor", lambda e, c5=c5, kk=kk: e.matmul(
                        ps_sm[0:1, 128 * c5:128 * c5 + 128],
                        onesb[:, :], vtm[:, 4 * c5 + kk, :],
                        start=(kk == 0), stop=(kk == 3)),
                        inc=(S["vt"] if (c5 == 7 and kk == 3) else None))
            kb.wait("vector", S["vt"])
            kb.wait("vector", S["pv"])
            kb.do("vector", lambda e: e.tensor_copy(
                trow[0:1, :], ps_sm[0:1, :]), inc=S["vtev"])
            evb["sm_read"] = S["vtev"].v
            kb.do("vector", lambda e: e.memset(g32[0:1, 896:1024], 0.0))
            for gi in range(6, -1, -1):
                kb.do("vector", lambda e, gi=gi: e.tensor_add(
                    g32[0:1, 128 * gi:128 * gi + 128],
                    g32[0:1, 128 * (gi + 1):128 * (gi + 1) + 128],
                    trow[0:1, 128 * (gi + 1):128 * (gi + 1) + 128]))
            kb.do("vector", lambda e: e.tensor_copy(
                gbf[0:1, :], g32[0:1, :]), inc=S["g"])
            g_at = S["g"].v

            def plan_S_mm(s, sub, sl):
                n_k = (2 * s + 1) if sub < 2 else (2 * s + 2)
                qt = 4 * s + sub
                chunks = [n_k - 1] + list(range(n_k - 1))
                sevv_f = S["sevV"].v + 1
                sevs_b = S["sevS"].v
                for ci, c in enumerate(chunks):
                    bk = BK_S[ci % 2]
                    if ci < 2:
                        kb.wait("tensor", S["sevS"], evb.get(("sS", bk), 0))
                        kb.wait("tensor", S["sevV"], evb.get(("sV", bk), 0))
                    elif ci == 2:
                        kb.wait("tensor", S["sevV"], sevv_f)
                    else:
                        kb.wait("tensor", S["sevS"], sevs_b + ci - 2)
                    kb.do("tensor", lambda e, hs=hs, qt=qt, c=c, bk=bk:
                          e.matmul(ps_b[:, bk, :],
                                   qbuf[:, hs, 128 * qt:128 * qt + 128],
                                   kbuf[:, hs, 512 * c:512 * c + 512],
                                   start=True, stop=True), inc=S["sps"])
                    if ci == 0:
                        evb[("sps_diag", sl)] = S["sps"].v
                        evb[("bk_diag", sl)] = bk
                    else:
                        evb[("sps_c", sl, ci)] = (S["sps"].v, bk)

            def plan_S_vec(s, sub, sl):
                base = 16 * sl
                rmax = cols[:, base:base + 1]
                negb = cols[:, base + 2:base + 3]
                kb.wait("vector", S["sps"], evb[("sps_diag", sl)])
                kb.wait("vector", S["exp"], evb.get(("ssb_free", sl), 0))
                bk = evb[("bk_diag", sl)]
                kb.do("vector", lambda e, bk=bk, sub=sub, sl=sl:
                      e.tensor_mul(Ssb[:, sl, :], ps_b[:, bk, :],
                                   maskb[:, sub % 2, :]), inc=S["sevV"])
                evb[("sV", bk)] = S["sevV"].v
                kb.do("vector", lambda e, sl=sl, rmax=rmax:
                      e.reduce_max(rmax, Ssb[:, sl, :], AX))
                kb.do("vector", lambda e, rmax=rmax, negb=negb:
                      e.tensor_scalar(negb, rmax, 0.0, -ISQ,
                                      OP.max, OP.mult), inc=S["mx"])
                evb[("mx_at", sl)] = S["mx"].v

            def plan_S_exp(s, sub, sl):
                n_k = (2 * s + 1) if sub < 2 else (2 * s + 2)
                base = 16 * sl
                negb = cols[:, base + 2:base + 3]
                zp_d = cols[:, base + 5:base + 6]
                kb.wait("scalar", S["mx"], evb[("mx_at", sl)])
                kb.wait("scalar", S["ptd"], evb.get(("psb_free", sl), 0))
                kb.wait("scalar", S["vn"], evb.get(("zp_free", sl), 0))
                c = n_k - 1
                kb.do("scalar", lambda e, sl=sl, c=c, negb=negb,
                      zp_d=zp_d: e.activation(
                          Psb[:, sl, 512 * c:512 * c + 512],
                          Ssb[:, sl, :], AF.Exp,
                          bias=negb, scale=ISQ, accum_out=zp_d))
                for ci in range(1, n_k):
                    c = ci - 1
                    spv, bk = evb[("sps_c", sl, ci)]
                    kb.wait("scalar", S["sps"], spv)
                    kb.do("scalar", lambda e, sl=sl, c=c, ci=ci, bk=bk,
                          negb=negb: e.activation(
                              Psb[:, sl, 512 * c:512 * c + 512],
                              ps_b[:, bk, :], AF.Exp,
                              bias=negb, scale=ISQ,
                              accum_out=cols[:, 16 * sl + 5 + ci:
                                             16 * sl + 6 + ci]),
                          inc=S["sevS"])
                    evb[("sS", bk)] = S["sevS"].v
                c_col = bcols[:, sl:sl + 1]
                kb.do("scalar", lambda e, negb=negb, c_col=c_col:
                      e.activation(c_col, negb, AF.Exp, bias=0.0,
                                   scale=1.0), inc=S["exp"])
                evb[("ssb_free", sl)] = S["exp"].v
                evb[("exp_at", sl)] = S["exp"].v

            def plan_soft(s, sub, sl):
                n_k = (2 * s + 1) if sub < 2 else (2 * s + 2)
                W = 512 * n_k
                nd = float(T - W)
                base = 16 * sl
                Z = cols[:, base + 3:base + 4]
                rz = cols[:, base + 4:base + 5]
                c_col = bcols[:, sl:sl + 1]
                kb.wait("vector", S["exp"], evb[("exp_at", sl)])
                for ci in range(1, n_k):
                    dst = Z if ci == n_k - 1 else cols[:, base + 5:base + 6]
                    kb.do("vector", lambda e, base=base, ci=ci, dst=dst:
                          e.tensor_add(dst, cols[:, base + 5:base + 6],
                                       cols[:, base + 5 + ci:
                                            base + 6 + ci]))
                if n_k == 1:
                    kb.do("vector", lambda e, base=base, Z=Z: e.tensor_copy(
                        Z, cols[:, base + 5:base + 6]))
                kb.do("vector", lambda e, nd=nd, c_col=c_col, Z=Z:
                      e.scalar_tensor_tensor(Z, c_col, nd, Z,
                                             OP.mult, OP.add))
                kb.do("vector", lambda e, Z=Z, rz=rz: e.reciprocal(rz, Z))
                kb.do("vector", lambda e, W=W, sl=sl, rz=rz:
                      e.tensor_single_scalar(Psb[:, sl, 0:W],
                                             Psb[:, sl, 0:W], rz, OP.mult))
                kb.wait("vector", S["ct"], evb.get("ct_at", 0))
                kb.do("vector", lambda e, sub=sub, c_col=c_col, rz=rz:
                      e.tensor_mul(bcols[:, 2 + sub:3 + sub], c_col, rz),
                      inc=S["vn"])
                evb[("zp_free", sl)] = S["vn"].v

            def plan_PT(s, sub, sl):
                n_k = (2 * s + 1) if sub < 2 else (2 * s + 2)
                vn_need = S["vn"].v
                for g in range(n_k):
                    bk = BK_T[g % 2]
                    kb.wait("tensor", S["vn"], vn_need)
                    kb.wait("tensor", S["vtev"], evb.get(("vt", bk), 0))
                    kb.wait("tensor", S["ptev"], evb.get(("pt", bk), 0))
                    if g == 0:
                        kb.wait("tensor", S["ctev"])
                    for kk in range(4):
                        kb.do("tensor", lambda e, g=g, kk=kk, bk=bk, sl=sl:
                              e.transpose(
                                  ps_bt[:, bk, 128 * kk:128 * kk + 128],
                                  Psb[:, sl, 512 * g + 128 * kk:
                                      512 * g + 128 * kk + 128],
                                  identb[:, :]),
                              inc=(S["pt"] if kk == 3 else None))
                    kb.wait("vector", S["pt"])
                    if g == 0:
                        kb.wait("vector", S["pv"], evb.get("pv_last", 0))
                    kb.do("vector", lambda e, g=g, sub=sub, bk=bk:
                          e.tensor_copy(
                              PT[:, 4 * g:4 * g + 4,
                                 128 * sub:128 * sub + 128],
                              ps_bt[:, bk, 0:512]), inc=S["ptev"])
                    evb[("pt", bk)] = S["ptev"].v
                kb.do("tensor", lambda e: e.drain(), inc=S["ptd"])
                evb[("psb_free", sl)] = S["ptd"].v

            def plan_stripe_end(s):
                n_lo, n_hi = 2 * s + 1, 2 * s + 2
                kb.do("vector", lambda e, n_lo=n_lo, n_hi=n_hi: e.memset(
                    PT[:, 4 * n_lo:4 * n_hi, 0:256], 0.0), inc=S["ptev"])
                pt_ready = S["ptev"].v
                kb.wait("tensor", S["vn"], S["vn"].v)
                kb.wait("tensor", S["vtev"], evb.get(("vt", 0), 0))
                kb.wait("tensor", S["ptev"], evb.get(("pt", 0), 0))
                kb.wait("tensor", S["ctev"])
                for sub in range(4):
                    kb.do("tensor", lambda e, sub=sub: e.transpose(
                        ps_bt[0:1, 0, 128 * sub:128 * sub + 128],
                        bcols[:, 2 + sub:3 + sub], identb[:, :]),
                        inc=(S["ct"] if sub == 3 else None))
                evb["ct_at"] = S["ct"].v
                kb.wait("vector", S["ct"])
                kb.do("vector", lambda e: e.tensor_copy(
                    crow[0:1, :], ps_bt[0:1, 0, 0:512]), inc=S["ctev"])
                bo = BK_O[s % 2]
                kb.wait("tensor", S["ptev"], pt_ready)
                kb.wait("tensor", S["o2"], evb.get(("o", bo), 0))
                kb.wait("tensor", S["g"], g_at)
                kb.wait("tensor", S["ctev"])
                for c in range(4 * n_hi):
                    kb.do("tensor", lambda e, c=c, bo=bo: e.matmul(
                        ps_b[:, bo, :], vtm[:, c, :], PT[:, c, :],
                        start=(c == 0), stop=False))
                gi1 = n_lo - 1
                st1 = (n_hi == 8)
                kb.do("tensor", lambda e, gi1=gi1, bo=bo, st1=st1: e.matmul(
                    ps_b[:, bo, 0:256],
                    gbf[0:1, 128 * gi1:128 * gi1 + 128],
                    crow[0:1, 0:256], start=False, stop=st1),
                    inc=(S["pv"] if st1 else None))
                if not st1:
                    gi2 = n_hi - 1
                    kb.do("tensor", lambda e, gi2=gi2, bo=bo: e.matmul(
                        ps_b[:, bo, 256:512],
                        gbf[0:1, 128 * gi2:128 * gi2 + 128],
                        crow[0:1, 256:512], start=False, stop=True),
                        inc=S["pv"])
                evb["pv_last"] = S["pv"].v
                kb.wait("vector", S["pv"])
                kb.do("vector", lambda e, h=h, s=s, bo=bo: e.tensor_copy(
                    out2[:, h, 512 * s:512 * s + 512], ps_b[:, bo, :]),
                    inc=S["o2"])
                evb[("o", bo)] = S["o2"].v

            jg0 = evb.get("jg", 0)
            for j in range(17):
                if j % 2 == 0:
                    plan_absum_slice(1)
                if j < 16:
                    plan_S_mm(j // 4, j % 4, (jg0 + j) % 2)
                    plan_S_vec(j // 4, j % 4, (jg0 + j) % 2)
                    plan_S_exp(j // 4, j % 4, (jg0 + j) % 2)
                if j >= 1:
                    pj = j - 1
                    plan_soft(pj // 4, pj % 4, (jg0 + pj) % 2)
                    plan_PT(pj // 4, pj % 4, (jg0 + pj) % 2)
                    if pj % 4 == 3:
                        plan_stripe_end(pj // 4)
            evb["jg"] = jg0 + 16
            kb.do("tensor", lambda e: e.drain(), inc=S["bfree"])
            bfree_at[0] = S["bfree"].v

        while abs_idx[0] < len(abs_jobs):
            plan_absum_slice()


def _phase_C1a(kb, S, sb, pst, st_, out2, wo_t, x_own_t, r1_sp):
    with ExitStack() as ces:
        wob = sb(ces, "wob", [128, 16, 2048], BF16)
        xc = sb(ces, "xc", [128, 2, 512], F32)
        r1c = sb(ces, "r1c", [128, 2, 512], F32)
        ps_c1 = pst(ces, "ps_c1", [128, 2, 512])
        evb = {}

        for f in range(16):
            kb.do("gpsimd", lambda e, f=f: e.dma_start(
                out=wob[:, f, :], in_=wo_t[f]), inc=S["w"], amt=16)
        kb.wait("tensor", S["w"])

        for st in range(4):
            for f in range(16):
                fs = f % 2
                kb.wait("tensor", S["ev"], evb.get(("wps", fs), 0))
                for d in range(16):
                    kb.do("tensor", lambda e, f=f, d=d, st=st, fs=fs:
                          e.matmul(ps_c1[:, fs, :],
                                   wob[:, f, 128 * d:128 * d + 128],
                                   out2[:, d, 512 * st:512 * st + 512],
                                   start=(d == 0), stop=(d == 15)),
                          inc=(S["mmps"] if d == 15 else None))
                kb.wait("sync", S["ev"], evb.get(("xf", fs), 0))
                kb.dma("sync", xc[:, fs, 0:256],
                       x_own_t[2 * st, :, f, :], S["xc%d" % fs])
                kb.dma("sync", xc[:, fs, 256:512],
                       x_own_t[2 * st + 1, :, f, :], S["xc%d" % fs])
                kb.wait("vector", S["mmps"])
                kb.wait("vector", S["xc%d" % fs])
                kb.wait("vector", S["r1d%d" % fs], evb.get(("rd", fs), 0))
                kb.do("vector", lambda e, fs=fs: e.tensor_add(
                    r1c[:, fs, :], ps_c1[:, fs, :], xc[:, fs, :]),
                    inc=S["ev"])
                evb[("wps", fs)] = S["ev"].v
                evb[("xf", fs)] = S["ev"].v
                kb.do("vector", lambda e: e.drain(), inc=S["r1"])
                kb.wait("sync", S["r1"])
                kb.dma("sync", r1_sp[f][:, 512 * st:512 * st + 512],
                       r1c[:, fs, :], S["r1d%d" % fs])
                evb[("rd", fs)] = S["r1d%d" % fs].v
        kb.settle([S["r1d0"], S["r1d1"]])


def _phase_C1b(kb, S, sb, pst, st_, h2q, sball, wscol, onesb, onesr,
               r1_sp, plan_ln):
    with ExitStack() as ces:
        h2buf = sb(ces, "h2buf", [128, 16, 512], F32)
        cscr = {
            "xb": sb(ces, "c_xb", [128, 2, 512], BF16),
            "sqb": sb(ces, "c_sqb", [128, 2, 512], BF16),
            "t1": sb(ces, "c_t1", [128, 512], F32)[:, :],
        }
        crows = sb(ces, "c_rows", [1, 5, 512], F32)
        habs = sb(ces, "habs", [128, 2, 512], F32)
        M1 = sb(ces, "M1", [128, 512], F32)
        rsb = sb(ces, "rsb", [128, 512], F32)
        qt1 = sb(ces, "qt1", [128, 2, 512], F32)
        ps_c = pst(ces, "ps_c", [128, 4, 512])
        evb = {}

        kb.do("vector", lambda e: e.tensor_scalar(
            crows[0:1, 4, 0:1], onesr[0:1, 0:1], 1.0, TWO_M25,
            OP.mult, OP.add))
        kb.do("vector", lambda e: e.tensor_scalar(
            crows[0:1, 4, 0:1], crows[0:1, 4, 0:1], 1.0, -1.0,
            OP.mult, OP.add))
        kb.do("vector", lambda e: e.tensor_scalar(
            crows[0:1, 4, 0:1], crows[0:1, 4, 0:1], MAGK, MAG32,
            OP.mult, OP.add), inc=S["vrow"])
        kb.wait("tensor", S["vrow"])
        kb.do("tensor", lambda e: e.matmul(
            ps_c[:, 0, 0:1], onesr[0:1, :], crows[0:1, 4, 0:1],
            start=True, stop=True), inc=S["mmps"])
        kb.wait("vector", S["mmps"])
        kb.do("vector", lambda e: e.tensor_copy(
            wscol[:, 14:15], ps_c[:, 0, 0:1]), inc=S["acc"])
        kb.wait("tensor", S["acc"])

        for st in range(4):
            for f in range(16):
                fs = f % 2
                kb.wait("sync", S["h2q"], evb.get(("hb", f), 0))
                kb.wait("sync", S["rr%d" % fs], S["rr%d" % fs].v)
                kb.dma("sync", h2buf[:, f, :],
                       r1_sp[f][:, 512 * st:512 * st + 512], S["rr%d" % fs])
                kb.wait("vector", S["rr%d" % fs])
                kb.wait("scalar", S["rr%d" % fs])
            gc = lambda i: h2buf[:, i, :]

            def wr_h2(e, i, t1, rps):
                return e.tensor_mul(h2buf[:, i, :], t1, rps)
            plan_ln(gc, 512, wr_h2, cscr, crows,
                    ps_c[0:1, 2, 0:512], ps_c[0:1, 3, 0:512],
                    ps_c[:, 0, :], ps_c[:, 1, :], evb, "c1b")
            ab_at = []
            for f in range(16):
                if f >= 2:
                    kb.wait("scalar", S["sqc"], ab_at[f - 2])
                kb.wait("scalar", S["h1"])
                kb.do("scalar", lambda e, f=f, b=habs[:, f % 2, :]:
                      e.activation(b, h2buf[:, f, :], AF.Abs), inc=S["sq"])
                kb.wait("vector", S["sq"])
                if f == 0:
                    kb.do("vector", lambda e: e.tensor_copy(
                        M1[:, :], habs[:, 0, :]), inc=S["sqc"])
                else:
                    kb.do("vector", lambda e, f=f: e.tensor_max(
                        M1[:, :], M1[:, :], habs[:, f % 2, :]),
                        inc=S["sqc"])
                ab_at.append(S["sqc"].v)
                if f == 15:
                    kb.do("vector", lambda e: e.drain(), inc=S["acc"])
            kb.wait("gpsimd", S["acc"])
            kb.do("gpsimd", lambda e: e.tensor_reduce(
                crows[0:1, 4, :], M1[:, :], AXC, OP.max), inc=S["par"])
            kb.wait("vector", S["par"])
            kb.do("vector", lambda e: e.tensor_scalar(
                crows[0:1, 4, :], crows[0:1, 4, :], 1.0 / 127.0, EPS,
                OP.mult, OP.add), inc=S["vrow"])
            kb.wait("tensor", S["vrow"])
            kb.wait("tensor", S["h1"])
            kb.wait("tensor", S["h2q"], evb.get("sbank", 0))
            kb.do("tensor", lambda e: e.matmul(
                ps_c[:, 0, :], onesr[0:1, :], crows[0:1, 4, :],
                start=True, stop=True), inc=S["mmps"])
            kb.wait("vector", S["mmps"])
            kb.do("vector", lambda e, st=st: e.tensor_copy(
                sball[:, st, :], ps_c[:, 0, :]))
            kb.do("vector", lambda e, st=st: e.reciprocal(
                rsb[:, :], sball[:, st, :]))
            for f in range(16):
                kb.do("vector", lambda e, f=f: e.tensor_mul(
                    qt1[:, 0, :], h2buf[:, f, :], rsb[:, :]))
                kb.do("vector", lambda e: e.tensor_scalar(
                    qt1[:, 0, :], qt1[:, 0, :], -128.0, 127.0,
                    OP.max, OP.min))
                kb.do("vector", lambda e: e.tensor_single_scalar(
                    qt1[:, 1, :], qt1[:, 0, :], wscol[:, 14:15], OP.add))
                kb.do("vector", lambda e, f=f, st=st: e.tensor_single_scalar(
                    h2q[:, f, 512 * st:512 * st + 512], qt1[:, 1, :],
                    wscol[:, 14:15], OP.subtract), inc=S["h2q"])
                evb[("hb", f)] = S["h2q"].v
            evb["sbank"] = S["h2q"].v
        h2q_at = S["h2q"].v
        return h2q_at


def _phase_C2(kb, S, sb, pst, st_, h2q, sbg, sbu, sball, rs2b, s2bd, wacc,
              wscol, onesr, identf, gmax, M16, wg_t, wu_t, wd_t, gu_sp,
              h2q_at):
    with ExitStack() as ces:
        wgb = sb(ces, "wgb", [128, 2, 2048], F32)
        wub = sb(ces, "wub", [128, 2, 2048], F32)
        wqg = sb(ces, "wqg", [128, 2048], BF16)
        wqu = sb(ces, "wqu", [128, 2048], BF16)
        g1t = sb(ces, "g1t", [128, 2048], BF16)
        g1u = sb(ces, "g1u", [128, 2048], BF16)
        tmp1 = sb(ces, "tmp1", [128, 2, 512], F32)
        tmp2 = sb(ces, "tmp2", [128, 2, 512], F32)
        gsil = sb(ces, "gsil", [128, 2, 512], F32)
        gust = sb(ces, "gust", [128, 2, 512], F32)
        gabs = sb(ces, "gabs", [128, 2, 512], F32)
        M2 = sb(ces, "M2", [128, 4, 512], F32)
        wdab = sb(ces, "wdab", [128, 2048], F32)
        tcol = sb(ces, "tcol", [128, 1], F32)
        srow = sb(ces, "srow", [1, 512], F32)
        ps_c2 = pst(ces, "ps_c2", [128, 4, 512])
        ps_cx = pst(ces, "ps_cx", [128, 512])
        evb = {}

        kb.do("vector", lambda e: e.memset(M2[:, :, :], 0.0))
        kb.wait("tensor", S["h2q"], h2q_at)

        def plan_ws_final(acol, base, hedged=False):
            kb.wait("vector", S["wabf"])
            w = 32
            while w >= 1:
                kb.do("vector", lambda e, acol=acol, w=w: e.tensor_add(
                    wacc[:, acol:acol + w], wacc[:, acol:acol + w],
                    wacc[:, acol + w:acol + 2 * w]))
                w //= 2
            kb.do("vector", lambda e: e.drain(), inc=S["vrow"])
            kb.wait("tensor", S["vrow"])
            kb.wait("tensor", S["acc"], S["acc"].v)
            kb.do("tensor", lambda e, acol=acol: e.transpose(
                ps_cx[0:1, 0:128], wacc[:, acol:acol + 1], identf[:, :]),
                inc=S["mmps"])
            kb.wait("vector", S["mmps"])
            kb.do("vector", lambda e: e.tensor_copy(
                srow[0:1, 0:128], ps_cx[0:1, 0:128]), inc=S["acc"])
            kb.do("vector", lambda e: e.memset(srow[0:1, 128:256], 0.0))
            k = 64
            while k >= 1:
                x = lambda a, b: srow[0:1, a:b]
                kb.do("vector", lambda e, k=k: e.tensor_add(
                    srow[0:1, 256:256 + k], x(0, k), x(k, 2 * k)))
                kb.do("vector", lambda e, k=k: e.tensor_sub(
                    srow[0:1, 320:320 + k], srow[0:1, 256:256 + k],
                    x(0, k)))
                kb.do("vector", lambda e, k=k: e.tensor_sub(
                    srow[0:1, 384:384 + k], srow[0:1, 256:256 + k],
                    srow[0:1, 320:320 + k]))
                kb.do("vector", lambda e, k=k: e.tensor_sub(
                    srow[0:1, 384:384 + k], x(0, k),
                    srow[0:1, 384:384 + k]))
                kb.do("vector", lambda e, k=k: e.tensor_sub(
                    srow[0:1, 448:448 + k], x(k, 2 * k),
                    srow[0:1, 320:320 + k]))
                kb.do("vector", lambda e, k=k: e.tensor_add(
                    srow[0:1, 384:384 + k], srow[0:1, 384:384 + k],
                    srow[0:1, 448:448 + k]))
                kb.do("vector", lambda e, k=k: e.tensor_add(
                    srow[0:1, 128:128 + k], srow[0:1, 128:128 + k],
                    srow[0:1, 128 + k:128 + 2 * k]))
                kb.do("vector", lambda e, k=k: e.tensor_add(
                    srow[0:1, 128:128 + k], srow[0:1, 128:128 + k],
                    srow[0:1, 384:384 + k]))
                kb.do("vector", lambda e, k=k: e.tensor_copy(
                    x(0, k), srow[0:1, 256:256 + k]))
                k //= 2
            kb.do("vector", lambda e: e.tensor_add(
                srow[0:1, 0:1], srow[0:1, 0:1], srow[0:1, 128:129]))
            nbc = 1
            if hedged:
                nbc = 3
                kb.do("vector", lambda e: e.tensor_scalar(
                    srow[0:1, 0:1], srow[0:1, 0:1], 1.0 / NW_GU, EPS,
                    OP.mult, OP.add))
                kb.do("vector", lambda e: e.tensor_scalar(
                    srow[0:1, 4:5], onesr[0:1, 0:1], 1.0, TWO_M25,
                    OP.mult, OP.add))
                kb.do("vector", lambda e: e.tensor_scalar(
                    srow[0:1, 4:5], srow[0:1, 4:5], 1.0, -1.0,
                    OP.mult, OP.add))
                kb.do("vector", lambda e: e.tensor_scalar(
                    srow[0:1, 4:5], srow[0:1, 4:5], -WBANDC, WBAND,
                    OP.mult, OP.add))
                kb.do("vector", lambda e: e.tensor_scalar(
                    srow[0:1, 1:2], srow[0:1, 4:5], -1.0, 0.5,
                    OP.mult, OP.add))
                kb.do("vector", lambda e: e.tensor_scalar(
                    srow[0:1, 2:3], srow[0:1, 4:5], 1.0, 0.5,
                    OP.mult, OP.add))
                kb.do("vector", lambda e: e.tensor_mul(
                    srow[0:1, 1:2], srow[0:1, 1:2], srow[0:1, 0:1]))
                kb.do("vector", lambda e: e.tensor_mul(
                    srow[0:1, 2:3], srow[0:1, 2:3], srow[0:1, 0:1]),
                    inc=S["vrow"])
            else:
                kb.do("vector", lambda e: e.tensor_scalar(
                    srow[0:1, 0:1], srow[0:1, 0:1], 1.0 / NW_GU, EPS,
                    OP.mult, OP.add), inc=S["vrow"])
            kb.wait("tensor", S["vrow"])
            kb.wait("tensor", S["acc"], S["acc"].v)
            kb.do("tensor", lambda e, nbc=nbc: e.matmul(
                ps_cx[:, 0:nbc], onesr[0:1, :], srow[0:1, 0:nbc],
                start=True, stop=True), inc=S["mmps"])
            kb.wait("vector", S["mmps"])
            if hedged:
                kb.do("vector", lambda e, base=base: e.tensor_copy(
                    wscol[:, base:base + 1], ps_cx[:, 0:1]))
                kb.do("vector", lambda e, base=base: e.tensor_copy(
                    wscol[:, base + 4:base + 5], ps_cx[:, 1:2]))
                kb.do("vector", lambda e, base=base: e.tensor_copy(
                    wscol[:, base + 5:base + 6], ps_cx[:, 2:3]),
                    inc=S["acc"])
            else:
                kb.do("vector", lambda e, base=base: e.tensor_copy(
                    wscol[:, base:base + 1], ps_cx[:, 0:1]), inc=S["acc"])
            kb.do("vector", lambda e, base=base: e.tensor_scalar_mul(
                wscol[:, base + 1:base + 2], wscol[:, base:base + 1], 0.5))
            if not hedged:
                kb.do("vector", lambda e, base=base: e.tensor_scalar_mul(
                    wscol[:, base + 2:base + 3], wscol[:, base:base + 1],
                    -0.5))
            kb.do("vector", lambda e, base=base: e.tensor_scalar_mul(
                wscol[:, base + 3:base + 4], wscol[:, base:base + 1], -1.0),
                inc=S["abs"])

        plan_ws_final(0, 0)
        plan_ws_final(64, 4)
        kb.wait("vector", S["abs"], 2)
        for st in range(4):
            kb.do("vector", lambda e, st=st: e.tensor_single_scalar(
                sbg[:, st, :], sball[:, st, :], wscol[:, 3:4], OP.mult))
            kb.do("vector", lambda e, st=st: e.tensor_single_scalar(
                sbu[:, st, :], sball[:, st, :], wscol[:, 7:8], OP.mult))

        wd_jobs = [("abs", o2, q) for o2 in range(16) for q in range(4)]
        wd_jobs.append(("fin",))
        wd_idx = [0]

        def plan_wd_job():
            if wd_idx[0] >= len(wd_jobs):
                return
            job = wd_jobs[wd_idx[0]]
            wd_idx[0] += 1
            if job[0] == "abs":
                _, o2, q = job
                kb.wait("sync", S["wabf"], S["wabf"].v)
                kb.dma("sync", wdab[:, :],
                       wd_t[o2][:, 16 * q:16 * q + 16, :], S["wab0"])
                kb.wait("scalar", S["wab0"])
                col = 128 + 4 * o2 + q
                kb.do("scalar", lambda e, col=col: e.activation(
                    wdab[:, :], wdab[:, :], AF.Abs,
                    accum_out=wacc[:, col:col + 1]), inc=S["wabf"])
            elif job[0] == "fin":
                plan_ws_final(128, 8, hedged=True)

        for o in range(64):
            os_ = o % 2
            kb.wait("sync", S["q2"], evb.get(("wg", os_), 0))
            kb.wait("sync", S["q2g"], evb.get(("wu", os_), 0))
            kb.dma("sync", wgb[:, os_, :], wg_t[o], S["wg%d" % os_])
            at_wg = S["wg%d" % os_].v
            kb.dma("sync", wub[:, os_, :], wu_t[o], S["wu%d" % os_])
            at_wu = S["wu%d" % os_].v
            plan_wd_job()
            plan_wd_job()
            kb.wait("vector", S["wg%d" % os_], at_wg)
            kb.wait("vector", S["abs"], 2)
            kb.wait("vector", S["gps"], evb.get("wqg_free", 0))
            kb.do("vector", lambda e, os_=os_: e.tensor_single_scalar(
                g1t[:, :], wgb[:, os_, :], wscol[:, 1:2], OP.is_ge))
            kb.do("vector", lambda e, os_=os_: e.scalar_tensor_tensor(
                wqg[:, :], wgb[:, os_, :], wscol[:, 2:3], g1t[:, :],
                OP.is_le, OP.subtract), inc=S["q2"])
            evb[("wg", os_)] = S["q2"].v
            kb.wait("vector", S["wu%d" % os_], at_wu)
            kb.wait("vector", S["ups"], evb.get("wqu_free", 0))
            kb.do("vector", lambda e, os_=os_: e.tensor_single_scalar(
                g1u[:, :], wub[:, os_, :], wscol[:, 5:6], OP.is_ge))
            kb.do("vector", lambda e, os_=os_: e.scalar_tensor_tensor(
                wqu[:, :], wub[:, os_, :], wscol[:, 6:7], g1u[:, :],
                OP.is_le, OP.subtract), inc=S["q2g"])
            evb[("wu", os_)] = S["q2g"].v
            kb.wait("tensor", S["q2"])
            kb.wait("tensor", S["q2g"])
            for st in range(4):
                sp_ = st % 2
                kb.wait("tensor", S["gfree"], evb.get(("pg", sp_), 0))
                for i in range(16):
                    kb.do("tensor", lambda e, i=i, st=st, sp_=sp_: e.matmul(
                        ps_c2[:, sp_, :], wqg[:, 128 * i:128 * i + 128],
                        h2q[:, i, 512 * st:512 * st + 512],
                        start=(i == 0), stop=(i == 15)),
                        inc=(S["gps"] if i == 15 else None))
                if st == 3:
                    evb["wqg_free"] = S["gps"].v
                kb.wait("tensor", S["ufree"], evb.get(("pu", sp_), 0))
                for i in range(16):
                    kb.do("tensor", lambda e, i=i, st=st, sp_=sp_: e.matmul(
                        ps_c2[:, 2 + sp_, :], wqu[:, 128 * i:128 * i + 128],
                        h2q[:, i, 512 * st:512 * st + 512],
                        start=(i == 0), stop=(i == 15)),
                        inc=(S["ups"] if i == 15 else None))
                if st == 3:
                    evb["wqu_free"] = S["ups"].v
                kb.wait("vector", S["gps"])
                kb.wait("vector", S["sil"], evb.get(("t1", sp_), 0))
                kb.do("vector", lambda e, st=st, sp_=sp_: e.tensor_mul(
                    tmp1[:, sp_, :], ps_c2[:, sp_, :], sbg[:, st, :]),
                    inc=S["gfree"])
                evb[("pg", sp_)] = S["gfree"].v
                kb.wait("scalar", S["gfree"])
                kb.wait("scalar", S["guv"], evb.get(("gsil", sp_), 0))
                kb.do("scalar", lambda e, sp_=sp_: e.activation(
                    gsil[:, sp_, :], tmp1[:, sp_, :], AF.Silu),
                    inc=S["sil"])
                evb[("t1", sp_)] = S["sil"].v
                kb.wait("vector", S["ups"])
                kb.do("vector", lambda e, st=st, sp_=sp_: e.tensor_mul(
                    tmp2[:, sp_, :], ps_c2[:, 2 + sp_, :], sbu[:, st, :]),
                    inc=S["ufree"])
                evb[("pu", sp_)] = S["ufree"].v
                kb.wait("vector", S["sil"])
                kb.wait("vector", S["gud%d" % sp_], evb.get(("gs", sp_), 0))
                kb.do("vector", lambda e, sp_=sp_: e.tensor_mul(
                    gust[:, sp_, :], gsil[:, sp_, :], tmp2[:, sp_, :]),
                    inc=S["guv"])
                evb[("gsil", sp_)] = S["guv"].v
                kb.wait("scalar", S["guv"])
                kb.wait("scalar", S["gu"], evb.get(("ga", sp_), 0))
                kb.do("scalar", lambda e, sp_=sp_: e.activation(
                    gabs[:, sp_, :], gust[:, sp_, :], AF.Abs), inc=S["sq"])
                kb.wait("vector", S["sq"])
                kb.do("vector", lambda e, st=st, sp_=sp_: e.tensor_max(
                    M2[:, st, :], M2[:, st, :], gabs[:, sp_, :]))
                if st == 0:
                    kb.do("vector", lambda e, o=o, sp_=sp_: e.reduce_max(
                        gmax[:, o:o + 1], gabs[:, sp_, :], AX),
                        inc=S["gu"])
                else:
                    kb.do("vector", lambda e, sp_=sp_: e.reduce_max(
                        tcol[:, :], gabs[:, sp_, :], AX))
                    kb.do("vector", lambda e, o=o: e.tensor_max(
                        gmax[:, o:o + 1], gmax[:, o:o + 1], tcol[:, :]),
                        inc=S["gu"])
                evb[("ga", sp_)] = S["gu"].v
                kb.wait("sync", S["guv"])
                kb.dma("sync", gu_sp[o][:, 512 * st:512 * st + 512],
                       gust[:, sp_, :], S["gud%d" % sp_])
                evb[("gs", sp_)] = S["gud%d" % sp_].v
        while wd_idx[0] < len(wd_jobs):
            plan_wd_job()
        kb.settle([S["gud0"], S["gud1"]])
        kb.do("vector", lambda e: e.tensor_single_scalar(
            wacc[:, 0:64], gmax[:, :], wscol[:, 8:9], OP.mult))
        kb.do("vector", lambda e: e.tensor_single_scalar(
            wacc[:, 64:128], wacc[:, 0:64], HEDGE_LO, OP.is_ge))
        kb.do("vector", lambda e: e.tensor_single_scalar(
            M16[:, :], wacc[:, 0:64], HEDGE_HI, OP.is_le))
        kb.do("vector", lambda e: e.tensor_mul(
            M16[:, :], M16[:, :], wacc[:, 64:128]), inc=S["gmx"])
        kb.wait("tensor", S["gu"], S["gu"].v)
        kb.wait("vector", S["abs"], 3)
        kb.wait("tensor", S["gfree"], S["gfree"].v)
        kb.wait("tensor", S["ufree"], S["ufree"].v)
        for st in range(4):
            kb.wait("tensor", S["acc"], S["acc"].v)
            for c4 in range(4):
                kb.do("tensor", lambda e, st=st, c4=c4: e.transpose(
                    ps_c2[:, c4, 0:128], M2[:, st, 128 * c4:128 * c4 + 128],
                    identf[:, :]), inc=(S["mmps"] if c4 == 3 else None))
            kb.wait("vector", S["mmps"])
            for c4 in range(4):
                kb.do("vector", lambda e, c4=c4: e.reduce_max(
                    wacc[:, 128 + c4:129 + c4], ps_c2[:, c4, 0:128], AX))
            kb.do("vector", lambda e: e.drain(), inc=S["vrow"])
            kb.wait("tensor", S["vrow"])
            for c4 in range(4):
                kb.do("tensor", lambda e, c4=c4: e.transpose(
                    ps_cx[0:1, 128 * c4:128 * c4 + 128],
                    wacc[:, 128 + c4:129 + c4],
                    identf[:, :]), inc=(S["mmps"] if c4 == 3 else None))
            kb.wait("vector", S["mmps"])
            kb.do("vector", lambda e: e.tensor_copy(
                srow[0:1, :], ps_cx[0:1, 0:512]))
            kb.do("vector", lambda e: e.tensor_scalar(
                srow[0:1, :], srow[0:1, :], 1.0 / 127.0, EPS,
                OP.mult, OP.add), inc=S["vrow"])
            kb.wait("tensor", S["vrow"])
            kb.wait("tensor", S["acc"], S["acc"].v)
            kb.do("tensor", lambda e: e.matmul(
                ps_cx[:, :], onesr[0:1, :], srow[0:1, :],
                start=True, stop=True), inc=S["mmps"])
            kb.wait("vector", S["mmps"])
            kb.do("vector", lambda e, st=st: e.tensor_copy(
                M2[:, st, :], ps_cx[:, :]), inc=S["acc"])
            kb.do("vector", lambda e, st=st: e.reciprocal(
                rs2b[:, st, :], M2[:, st, :]))
            kb.do("vector", lambda e, st=st: e.tensor_single_scalar(
                s2bd[:, st, :], M2[:, st, :], wscol[:, 11:12], OP.mult),
                inc=S["prep"])
    return S["prep"].v


def _phase_C3(kb, S, sb, pst, st_, rs2b, s2bd, gu_sp, wdq_sp, r1_sp, out_t,
              s2_at, wd_t, wscol, M16):
    with ExitStack() as ces:
        xq2 = sb(ces, "xq2", [128, 64, 1024], BF16)
        wdqb = sb(ces, "wdqb", [128, 8192], BF16)
        gub = sb(ces, "gub", [128, 2, 1024], F32)
        tq = sb(ces, "tq", [128, 1024], F32)
        r1c3 = sb(ces, "r1c3", [128, 512], F32)
        outst = sb(ces, "outst", [128, 512], F32)
        wdw = sb(ces, "wdw", [128, 2, 2048], F32)
        sgw = sb(ces, "sgw", [128, 1024], BF16)
        p0t = sb(ces, "p0t", [128, 1024], BF16)
        p1t = sb(ces, "p1t", [128, 1024], BF16)
        wdq3 = sb(ces, "wdq3", [128, 1024], BF16)
        ps_c3 = pst(ces, "ps_c3", [128, 2, 512])
        evb = {}

        q3_jobs = [(o2, q) for o2 in range(16) for q in range(4)]
        q3_idx = [0]

        def plan_wd_q():
            if q3_idx[0] >= len(q3_jobs):
                return
            o2, q = q3_jobs[q3_idx[0]]
            q3_idx[0] += 1
            wsl = q3_idx[0] % 2
            kb.wait("sync", S["wqv"], evb.get(("wfree", wsl), 0))
            kb.dma("sync", wdw[:, wsl, :],
                   wd_t[o2][:, 16 * q:16 * q + 16, :], S["wdb%d" % wsl])
            for h in range(2):
                hs = slice(1024 * h, 1024 * h + 1024)
                kb.wait("scalar", S["wdb%d" % wsl])
                kb.wait("scalar", S["wqv"], evb.get("sgread", 0))
                kb.do("scalar", lambda e, hs=hs, wsl=wsl: e.activation(
                    sgw[:, :], wdw[:, wsl, hs], AF.Sign, scale=-1.0))
                kb.do("scalar", lambda e, hs=hs, wsl=wsl: e.activation(
                    wdw[:, wsl, hs], wdw[:, wsl, hs], AF.Abs),
                    inc=S["wqs"])
                kb.wait("vector", S["wqs"])
                kb.wait("vector", S["gmx"])
                kb.wait("vector", S["wdqd"], evb.get("q3spill", 0))
                kb.do("vector", lambda e, hs=hs, wsl=wsl:
                      e.tensor_single_scalar(
                          p0t[:, :], wdw[:, wsl, hs], wscol[:, 9:10],
                          OP.is_ge))
                kb.do("vector", lambda e, hs=hs, wsl=wsl:
                      e.tensor_single_scalar(
                          p1t[:, :], wdw[:, wsl, hs], wscol[:, 12:13],
                          OP.is_ge))
                kb.do("vector", lambda e, hs=hs, wsl=wsl:
                      e.tensor_single_scalar(
                          wdq3[:, :], wdw[:, wsl, hs], wscol[:, 13:14],
                          OP.is_ge), inc=S["wqv"])
                evb[("wfree", wsl)] = S["wqv"].v
                kb.do("vector", lambda e: e.tensor_add(
                    p1t[:, :], p1t[:, :], wdq3[:, :]))
                kb.do("vector", lambda e: e.scalar_tensor_tensor(
                    p1t[:, :], p1t[:, :], 0.5, p0t[:, :],
                    OP.mult, OP.subtract))
                for c in range(8):
                    co = 16 * q + 8 * h + c
                    kb.do("vector", lambda e, c=c, co=co:
                          e.tensor_single_scalar(
                              p1t[:, 128 * c:128 * c + 128],
                              p1t[:, 128 * c:128 * c + 128],
                              M16[:, co:co + 1], OP.mult))
                kb.do("vector", lambda e: e.tensor_add(
                    p0t[:, :], p0t[:, :], p1t[:, :]))
                kb.do("vector", lambda e: e.tensor_mul(
                    wdq3[:, :], p0t[:, :], sgw[:, :]), inc=S["wqv"])
                evb["sgread"] = S["wqv"].v
                kb.wait("sync", S["wqv"])
                kb.dma("sync",
                       wdq_sp[o2][:, 2048 * q + 1024 * h:
                                  2048 * q + 1024 * h + 1024],
                       wdq3[:, :], S["wdqd"])
                evb["q3spill"] = S["wdqd"].v
            if q == 3:
                evb[("wdone", o2)] = S["wdqd"].v

        kb.wait("vector", S["prep"], s2_at)
        for st2 in range(2):
            kb.wait("vector", S["dps"], S["dps"].v)
            for i2 in range(64):
                if st2 == 0 and i2 % 8 == 0:
                    plan_wd_q()
                isl = i2 % 2
                kb.wait("sync", S["xq2"], evb.get(("gb", isl), 0))
                kb.dma("sync", gub[:, isl, :],
                       gu_sp[i2][:, 1024 * st2:1024 * st2 + 1024],
                       S["gurd%d" % isl])
                kb.wait("vector", S["gurd%d" % isl])
                kb.do("vector", lambda e, isl=isl, st2=st2: e.tensor_mul(
                    tq[:, 0:512], gub[:, isl, 0:512], rs2b[:, 2 * st2, :]))
                kb.do("vector", lambda e, isl=isl, st2=st2: e.tensor_mul(
                    tq[:, 512:1024], gub[:, isl, 512:1024],
                    rs2b[:, 2 * st2 + 1, :]))
                kb.do("vector", lambda e: e.tensor_scalar(
                    tq[:, :], tq[:, :], -128.0, 127.0, OP.max, OP.min))
                kb.do("vector", lambda e: e.tensor_single_scalar(
                    tq[:, :], tq[:, :], wscol[:, 14:15], OP.add))
                kb.do("vector", lambda e, i2=i2: e.tensor_single_scalar(
                    xq2[:, i2, :], tq[:, :], wscol[:, 14:15], OP.subtract),
                    inc=S["xq2"])
                evb[("gb", isl)] = S["xq2"].v
            xq_done = S["xq2"].v
            for o2 in range(16):
                if st2 == 0:
                    while q3_idx[0] < min(len(q3_jobs), 4 * (o2 + 3)):
                        plan_wd_q()
                kb.wait("sync", S["dps"], evb.get("wd", 0))
                kb.wait("sync", S["wdqd"], evb.get(("wdone", o2), 0))
                wdl_at = []
                for qq in range(4):
                    if qq:
                        kb.wait("sync", S["wdl0"])
                    kb.dma("sync", wdqb[:, 2048 * qq:2048 * qq + 2048],
                           wdq_sp[o2][:, 2048 * qq:2048 * qq + 2048],
                           S["wdl0"])
                    wdl_at.append(S["wdl0"].v)
                kb.wait("tensor", S["xq2"], xq_done)
                for sub in range(2):
                    kb.wait("tensor", S["dfree"], evb.get(("pd", sub), 0))
                    for i2 in range(64):
                        if sub == 0 and i2 % 16 == 0:
                            kb.wait("tensor", S["wdl0"], wdl_at[i2 // 16])
                        kb.do("tensor", lambda e, i2=i2, sub=sub:
                              e.matmul(ps_c3[:, sub, :],
                                       wdqb[:, 128 * i2:128 * i2 + 128],
                                       xq2[:, i2, 512 * sub:512 * sub + 512],
                                       start=(i2 == 0), stop=(i2 == 63)),
                              inc=(S["dps"] if i2 == 63 else None))
                    if sub == 1:
                        evb["wd"] = S["dps"].v
                    kb.wait("sync", S["oev"], evb.get("rx", 0))
                    kb.dma("sync", r1c3[:, :],
                           r1_sp[o2][:, 1024 * st2 + 512 * sub:
                                     1024 * st2 + 512 * sub + 512],
                           S["rc0"])
                    kb.wait("vector", S["dps"])
                    kb.wait("vector", S["rc0"])
                    kb.wait("vector", S["ow0"], evb.get("os", 0))
                    kb.do("vector", lambda e, st2=st2, sub=sub: e.tensor_mul(
                        outst[:, :], ps_c3[:, sub, :],
                        s2bd[:, 2 * st2 + sub, :]), inc=S["dfree"])
                    evb[("pd", sub)] = S["dfree"].v
                    kb.do("vector", lambda e: e.tensor_add(
                        outst[:, :], outst[:, :], r1c3[:, :]), inc=S["oev"])
                    evb["rx"] = S["oev"].v
                    kb.wait("sync", S["oev"])
                    kb.dma("sync",
                           out_t[o2][:, 1024 * st2 + 512 * sub:
                                     1024 * st2 + 512 * sub + 512],
                           outst[:, :], S["ow0"])
                    evb["os"] = S["ow0"].v
        kb.wait("sync", S["ow0"])


def _emit(nc, kb):
    with nc.Block() as block:
        @block.sync
        def _(e):
            for f in kb.plan["sync"]:
                f(e)

        @block.gpsimd
        def _(e):
            for f in kb.plan["gpsimd"]:
                f(e)

        @block.tensor
        def _(e):
            for f in kb.plan["tensor"]:
                f(e)

        @block.vector
        def _(e):
            for f in kb.plan["vector"]:
                f(e)

        @block.scalar
        def _(e):
            for f in kb.plan["scalar"]:
                f(e)



def _tile_attn_w(w):
    return np.ascontiguousarray(
        w.reshape(16, 128, 16, 128).transpose(2, 1, 0, 3))


def _prep_core(xT, parity):
    own = [2 * t + parity for t in range(16)]
    cols = np.concatenate([np.arange(128 * j, 128 * j + 128) for j in own])
    xTo = xT[:, cols]
    x_own_t = np.ascontiguousarray(
        xTo.reshape(16, 128, 8, 256).transpose(2, 1, 0, 3))
    x_full_t = np.ascontiguousarray(
        xT.reshape(16, 128, 16, 256).transpose(2, 1, 0, 3))
    deltas = (128 * parity, 128 * parity + 256)
    m = np.zeros((2, 128, 512), dtype=np.float32)
    tt = np.arange(128)[:, None]
    cc = np.arange(512)[None, :]
    for k, dl in enumerate(deltas):
        m[k] = (cc <= tt + dl).astype(np.float32)
    m = np.ascontiguousarray(
        m.transpose(1, 0, 2)).astype(ml_dtypes.bfloat16)
    return x_own_t, x_full_t, m, cols


def kernel(L0_attn_x, L0_attn_Wq, L0_attn_Wk, L0_attn_Wv, L0_attn_Wo,
           L0_ffn_w_gate, L0_ffn_w_up, L0_ffn_w_down):
    x = np.ascontiguousarray(np.asarray(L0_attn_x, dtype=np.float32))
    wq = _tile_attn_w(np.asarray(L0_attn_Wq, dtype=np.float32))
    wk = _tile_attn_w(np.asarray(L0_attn_Wk, dtype=np.float32))
    wv = _tile_attn_w(np.asarray(L0_attn_Wv, dtype=np.float32))
    wo = _tile_attn_w(np.asarray(L0_attn_Wo, dtype=np.float32))
    wg = np.ascontiguousarray(np.asarray(L0_ffn_w_gate, dtype=np.float32)
                              .reshape(64, 128, 16, 128)
                              .transpose(0, 3, 2, 1))
    wu = np.ascontiguousarray(np.asarray(L0_ffn_w_up, dtype=np.float32)
                              .reshape(64, 128, 16, 128)
                              .transpose(0, 3, 2, 1))
    wd = np.ascontiguousarray(np.asarray(L0_ffn_w_down, dtype=np.float32)
                              .reshape(16, 128, 64, 128)
                              .transpose(0, 3, 2, 1))
    ident = np.eye(128, dtype=ml_dtypes.bfloat16)
    identf = np.eye(128, dtype=np.float32)
    ones = np.ones((128, 1), dtype=ml_dtypes.bfloat16)
    onesr = np.ones((1, 128), dtype=np.float32)

    nc = build_nc(7)
    in_maps = []
    core_cols = []
    for c in range(8):
        b, parity = c // 2, c % 2
        xT = np.ascontiguousarray(x[b].T)
        x_own_t, x_full_t, m, colsv = _prep_core(xT, parity)
        core_cols.append(colsv)
        in_maps.append({
            "x_own_t": x_own_t, "x_full_t": x_full_t,
            "wq_t": wq, "wk_t": wk, "wv_t": wv, "wo_t": wo,
            "wg_t": wg, "wu_t": wu, "wd_t": wd,
            "masks": m, "identc": ident, "identfc": identf,
            "onesc": ones, "onesrc": onesr,
        })
    res = run_bass_kernel_spmd(nc, in_maps, core_ids=list(range(8)))
    y = np.empty((4, 4096, 2048), dtype=np.float32)
    for c in range(8):
        b = c // 2
        o = np.asarray(res.results[c]["out_t"])
        y[b][core_cols[c], :] = o.reshape(2048, 2048).T
    return y
